# version 18
# baseline (speedup 1.0000x reference)
import numpy as np
import ml_dtypes
from contextlib import ExitStack

PUMP = True
OMM_TAIL = True
PUMP_CLASSES = {'n1', 'n2'}

import concourse.bass as bass
import concourse.mybir as mybir
from concourse.bass_utils import run_bass_kernel_spmd

F32 = mybir.dt.float32
F16 = mybir.dt.float16
BF16 = mybir.dt.bfloat16
AF = mybir.ActivationFunctionType
ALU = mybir.AluOpType

DM, DI, DS, DR = 1024, 2048, 16, 64
NDT = DI // 128
NMT = DM // 128
NOC = 2 * DI // 128
ALPHA = 0.1
EPS = 1e-5
MAC = 128


class Ctr:
    def __init__(self, es, nc, name):
        self.h = es.enter_context(nc.semaphore(name))
        self.n = 0

    def inc(self, inst, k=1):
        inst.then_inc(self.h, k)
        self.n += k
        return inst


class Bld:

    def __init__(self, nc, es):
        self.nc = nc
        self.block = es.enter_context(nc.Block())
        self.cv = Ctr(es, nc, "cv")
        self.cs = Ctr(es, nc, "cs")
        self.ct = Ctr(es, nc, "ct")
        self.cp = Ctr(es, nc, "cp")
        self.ql = Ctr(es, nc, "ql")
        self.qs = Ctr(es, nc, "qs")
        self.qg = Ctr(es, nc, "qg")
        self._pend = []
        self.reg = {}
        self.bar = None
        self.covered = {}

    def _deps_for(self, reads, writes):
        deps = []
        for bn in reads:
            e = self.reg.get(bn)
            if e and e["w"]:
                deps.append(e["w"])
        for bn in writes:
            e = self.reg.get(bn)
            if e:
                if e["w"]:
                    deps.append(e["w"])
                deps.extend(e["r"])
        if self.bar is not None:
            deps.append(self.bar)
        return deps

    def _post(self, tok, reads, writes):
        for bn in reads:
            self.reg.setdefault(bn, {"w": None, "r": []})["r"].append(tok)
        for bn in writes:
            self.reg[bn] = {"w": tok, "r": []}

    @staticmethod
    def _collapse(deps):
        m = {}
        for ctr, n in deps:
            if ctr is not None:
                m[id(ctr)] = (ctr, max(n, m.get(id(ctr), (ctr, 0))[1]))
        return list(m.values())

    def _emit_waits(self, eng, deps, stream=None):
        cov = self.covered.setdefault(stream, {}) if stream else None
        for ctr, n in self._collapse(deps):
            if cov is not None:
                if cov.get(id(ctr), -1) >= n:
                    continue
                cov[id(ctr)] = n
            eng.wait_ge(ctr.h, n)

    def _stage(self, emitfn, engfn, ctr, reads, writes, stream=None):
        if reads is None and writes is None:
            def body(e):
                cov = self.covered.setdefault(stream, {}) if stream else {}
                for c2 in (self.cv, self.cs, self.ct, self.cp,
                           self.ql, self.qs, self.qg):
                    e.wait_ge(c2.h, c2.n)
                    cov[id(c2)] = c2.n
                for c2, n2 in self._pend:
                    e.wait_ge(c2.h, n2)
                    cov[id(c2)] = max(cov.get(id(c2), -1), n2)
                self._pend = []
                ctr.inc(engfn(e))
            emitfn(body)
            self.bar = (ctr, ctr.n)
            for bn in list(self.reg):
                self.reg[bn] = {"w": self.bar, "r": []}
            return self.bar
        reads = reads or []
        writes = writes or []
        deps = self._deps_for(reads, writes) + self._pend
        self._pend = []

        def body(e):
            self._emit_waits(e, deps, stream)
            ctr.inc(engfn(e))
        emitfn(body)
        tok = (ctr, ctr.n)
        self._post(tok, reads, writes)
        return tok

    def ve(self, fn, r=None, w=None):
        return self._stage(self.block.vector, fn, self.cv, r, w, "v")

    def sc(self, fn, r=None, w=None):
        return self._stage(self.block.scalar, fn, self.cs, r, w, "s")

    def te(self, fn, r=None, w=None):
        return self._stage(self.block.tensor, fn, self.ct, r, w, "t")

    def gp(self, fn, r=None, w=None):
        return self._stage(self.block.gpsimd, fn, self.cp, r, w, "p")

    def load(self, out, in_, q=None, slow=False, after=(), r=None, w=None):
        ctr = q if q is not None else self.ql
        if r is None and w is None:
            cv_n, cs_n, ct_n = self.cv.n, self.cs.n, self.ct.n
            extra = tuple(after)

            def body(e):
                e.wait_ge(self.cv.h, cv_n)
                e.wait_ge(self.cs.h, cs_n)
                e.wait_ge(self.ct.h, ct_n)
                for c2, n2 in extra:
                    e.wait_ge(c2.h, n2)
                ctr.inc(e.dma_start(out=out, in_=in_,
                                    allow_slow_non_contiguous=slow), 16)
            if ctr is self.ql:
                self.block.sync(body)
            elif ctr is self.qg:
                self.block.gpsimd(body)
            else:
                self.block.scalar(body)
            return (ctr, ctr.n)
        r = r or []
        w = w or []
        deps = self._deps_for(r, w) + list(after)
        stream = "sync" if ctr is self.ql else ("g" if ctr is self.qg else "s")

        def body(e):
            self._emit_waits(e, deps, stream)
            ctr.inc(e.dma_start(out=out, in_=in_,
                                allow_slow_non_contiguous=slow), 16)
        if ctr is self.ql:
            self.block.sync(body)
        elif ctr is self.qg:
            self.block.gpsimd(body)
        else:
            self.block.scalar(body)
        tok = (ctr, ctr.n)
        self._post(tok, r or [], w or [])
        return tok

    def store(self, out, in_, slow=False, r=None, w=None):
        return self.load(out, in_, q=self.qs, slow=slow, r=r, w=w)

    def want(self, tok):
        self._pend.append(tok)

    def finish(self):
        qs_n, ql_n, qg_n = self.qs.n, self.ql.n, self.qg.n
        cv_n, cs_n, ct_n, cp_n = self.cv.n, self.cs.n, self.ct.n, self.cp.n

        def body(e):
            e.wait_ge(self.cv.h, cv_n)
            e.wait_ge(self.cs.h, cs_n)
            e.wait_ge(self.ct.h, ct_n)
            e.wait_ge(self.cp.h, cp_n)
            e.wait_ge(self.qs.h, qs_n)
            e.wait_ge(self.ql.h, ql_n)
            e.wait_ge(self.qg.h, qg_n)
        self.block.sync(body)


def build(T, TT=256, dbg=False):
    WIN0 = T // 2
    NT = T // TT
    TCH = TT // 128
    BL = TT + 2
    CBL = TT + 4
    NCH = T // MAC
    WCH0 = WIN0 // MAC

    nc = bass.Bass("TRN2", target_bir_lowering=False)

    def din(name, shape, dt):
        return nc.dram_tensor(name, list(shape), dt, kind="ExternalInput").ap()

    xin = din("xin", [T, DM], F32)
    mrow_16 = din("mrow_16", [1, T], F16)
    mcols = din("mcols", [128, T // 128], F32)
    in_wT = din("in_wT", [DM, 2 * DI], BF16)
    in_b_row = din("in_b_row", [1, 2 * DI], BF16)
    inb_c = din("inb_c", [128, NOC], F32)
    inb_r32 = din("inb_r32", [1, NOC * 128], F32)
    outb32 = din("outb32", [1, DM], F32)
    fb32 = din("fb32", [1, DM], F32)
    ones32 = din("ones32", [1, 128], F32)
    convw_c = din("convw_c", [128, NDT * 4], F32)
    convb_c = din("convb_c", [128, NDT], F32)
    xp_pad = din("xp_pad", [DI, 128], F32)
    dt_wTi = din("dt_wTi", [DR, DI], BF16)
    dtb_c = din("dtb_c", [128, NDT], F32)
    alog_r = din("alog_r", [128, NDT * DS], F32)
    d_c = din("d_c", [128, NDT], F32)
    nw1_c = din("nw1_c", [128, NMT], F32)
    nw2_c = din("nw2_c", [128, NMT], F32)
    out_wTi = din("out_wTi", [DI, DM], BF16)
    out_b_row = din("out_b_row", [1, DM], BF16)
    wW = din("wW", [DM, DM], F32)
    wR = din("wR", [DM, DM], F32)
    fW1 = din("fW1", [DM, DM], F32)
    fW2 = din("fW2", [DM, DM], F32)
    fW3 = din("fW3", [DM, DM], F32)
    fb_row = din("fb_row", [1, DM], BF16)
    decay = din("decay", [1, 1], F32)
    diffsT = din("diffsT", [MAC, MAC], F32)
    causT = din("causT", [MAC, MAC], F32)
    p_row = din("p_row", [1, MAC], F32)
    cp_col = din("cp_col", [MAC, 1], F32)
    ident_in = din("ident_in", [128, 128], BF16)
    ones_row = din("ones_row", [1, 128], BF16)

    o = nc.dram_tensor("o", [T - WIN0, DM], F32, kind="ExternalOutput").ap()

    x1sp = nc.dram_tensor("x1sp", [T, DM], F32, kind="Internal").ap()
    o2tm = nc.dram_tensor("o2tm", [T, DM], F32, kind="Internal").ap()
    o2fm = nc.dram_tensor("o2fm", [DM, T + 1], BF16, kind="Internal").ap()
    bsp = nc.dram_tensor("bsp", [1, DS, T], F16, kind="Internal").ap()
    csp = nc.dram_tensor("csp", [1, DS, T], F16, kind="Internal").ap()
    wlo = nc.dram_tensor("wlo", [5, DM, DM], BF16, kind="Internal").ap()
    dbg_names = []
    dbg_aps = {}
    if dbg:
        for nm, shape, dt in [("d_x1", [T, DM], F32), ("d_o2tm", [T, DM], F32),
                              ("d_o2fm", [DM, T + 1], BF16)]:
            dbg_aps[nm] = nc.dram_tensor(nm, shape, dt, kind="ExternalOutput").ap()
            dbg_names.append(nm)
        dbg_aps["dump"] = nc.dram_tensor("dump", [64, 128, 4224], F16,
                                         kind="ExternalOutput").ap()

    with ExitStack() as es:
        E = es.enter_context
        b = Bld(nc, es)
        dump_state = {"i": 0, "log": [], "buf": None}

        def DUMP(tag, ap):
            if not dbg or dump_state["buf"] is None:
                return
            dmp32 = dump_state["buf"]
            i = dump_state["i"]
            if i >= 64:
                return
            dump_state["i"] += 1
            dump_state["log"].append(tag)
            n = ap.shape[-1] if len(ap.shape) == 2 else ap.free_size()
            n = min(n, 2176)
            p = ap.shape[0]
            b.want(dump_state.get("tok") or (b.qs, 0))
            b.ve(lambda v: v.memset(dmp32[:], 0.0))
            b.want(dump_state.get("tok") or (b.qs, 0))
            b.ve(lambda v: v.tensor_copy(dmp32[0:p, 0:n], ap[:, 0:n] if len(ap.shape) == 2 else ap))
            dump_state["tok"] = b.store(dbg_aps["dump"][i, :, 0:2176], dmp32[:])

        mcol = E(nc.sbuf_tensor("mcol", [128, T // 128], F32))
        convw = E(nc.sbuf_tensor("convw", [128, NDT * 4], F32))
        convb = E(nc.sbuf_tensor("convb", [128, NDT], F32))

        dtw = E(nc.sbuf_tensor("dtw", [DR, DI], BF16))
        dtb = E(nc.sbuf_tensor("dtb", [128, NDT], F32))
        inbr = E(nc.sbuf_tensor("inbr", [1, NOC * 128], F32))
        ones2r = E(nc.sbuf_tensor("ones2r", [1, TT], F32))
        asb = E(nc.sbuf_tensor("asb", [128, NDT * DS], F32))
        dcol = E(nc.sbuf_tensor("dcol", [128, NDT], F32))
        nw1 = E(nc.sbuf_tensor("nw1", [128, NMT], F32))
        nw2 = E(nc.sbuf_tensor("nw2", [128, NMT], F32))
        inbc = E(nc.sbuf_tensor("inbc", [128, NOC], F32))
        outb2 = E(nc.sbuf_tensor("outb2", [1, DM], F32))
        fbr2 = E(nc.sbuf_tensor("fbr2", [1, DM], F32))
        ones3 = E(nc.sbuf_tensor("ones3", [1, 128], F32))
        ident = E(nc.sbuf_tensor("ident", [128, 128], BF16))
        ones1 = E(nc.sbuf_tensor("ones1", [1, 128], BF16))
        dk = E(nc.sbuf_tensor("dk", [1, 1], F32))
        gam = E(nc.sbuf_tensor("gam", [1, 1], F32))
        logg1 = E(nc.sbuf_tensor("logg1", [1, 1], F32))
        loggb = E(nc.sbuf_tensor("loggb", [1, 1], BF16))
        loggc = E(nc.sbuf_tensor("loggc", [128, 1], F32))
        gin_row = E(nc.sbuf_tensor("gin_row", [1, MAC], F32))
        gin_rowb = E(nc.sbuf_tensor("gin_rowb", [1, MAC], BF16))
        gin_bc = E(nc.sbuf_tensor("gin_bc", [128, MAC], BF16))
        gin_lo = E(nc.sbuf_tensor("gin_lo", [128, MAC], BF16))
        ginr32 = E(nc.sbuf_tensor("ginr32", [1, MAC], F32))
        gin_row2 = E(nc.sbuf_tensor("gin_row2", [1, MAC], F32))
        gw_col = E(nc.sbuf_tensor("gw_col", [MAC, 1], F32))
        gc_col = E(nc.sbuf_tensor("gc_col", [128, 1], F32))
        c128 = E(nc.sbuf_tensor("c128", [128, 1], F32))
        cpc = E(nc.sbuf_tensor("cpc", [MAC, 1], F32))
        dTt = E(nc.sbuf_tensor("dTt", [MAC, MAC], F32))
        cTt = E(nc.sbuf_tensor("cTt", [MAC, MAC], F32))
        mTt = E(nc.sbuf_tensor("mTt", [MAC, MAC], F32))
        gwm = E(nc.sbuf_tensor("gwm", [MAC, 1], F32))
        jsq = E(nc.sbuf_tensor("jsq", [128, DM], F32))
        col1 = E(nc.sbuf_tensor("col1", [128, 1], F32))
        col2 = E(nc.sbuf_tensor("col2", [128, 1], F32))
        col3 = E(nc.sbuf_tensor("col3", [128, 1], F32))
        col4 = E(nc.sbuf_tensor("col4", [128, 1], F32))
        col5 = E(nc.sbuf_tensor("col5", [128, 1], F32))
        zcol = E(nc.sbuf_tensor("zcol", [128, 1], BF16))
        epscol = E(nc.sbuf_tensor("epscol", [128, 1], F32))
        onecol = E(nc.sbuf_tensor("onecol", [128, 1], F32))
        zerocol = E(nc.sbuf_tensor("zerocol", [128, 1], F32))
        carry = E(nc.sbuf_tensor("carry", [128, NDT * DS], F16))

        if dbg and False:
            dump_state["buf"] = E(nc.sbuf_tensor("dmp32", [128, 2176], F16))
        es2 = ExitStack()
        E2 = es2.enter_context
        xpw = E2(nc.sbuf_tensor("xpw", [128, NDT * 128], BF16))
        xpl = E2(nc.sbuf_tensor("xpl", [128, NDT * 128], BF16))
        mtile = E2(nc.sbuf_tensor("mtile", [128, 2 * TT], F16))
        xtm = E2(nc.sbuf_tensor("xtm", [128, DM], F32))
        xrfm = E2(nc.sbuf_tensor("xrfm", [128, NMT * TT], BF16))
        wstage = E2(nc.sbuf_tensor("wstage", [128, NMT * 128], BF16))
        wstage2 = E2(nc.sbuf_tensor("wstage2", [128, NMT * 128], BF16))
        ostage = E2(nc.sbuf_tensor("ostage", [128, DM], BF16))
        ostage2 = E2(nc.sbuf_tensor("ostage2", [128, DM], BF16))
        xzc = E2(nc.sbuf_tensor("xzc", [128, NDT * CBL], F16))
        silu_r = E2(nc.sbuf_tensor("silu_r", [128, NDT * TT], F16))
        xm16 = E2(nc.sbuf_tensor("xm16", [128, NDT * TT], F16))
        xmbf = E2(nc.sbuf_tensor("xmbf", [128, NDT * TT], BF16))
        xmlo = E2(nc.sbuf_tensor("xmlo", [128, NDT * TT], BF16))
        del32 = E2(nc.sbuf_tensor("del32", [128, NDT * TT], F32))
        dtmp = E2(nc.sbuf_tensor("dtmp", [128, TT], F16))
        xb16 = E2(nc.sbuf_tensor("xb16", [DS, TT], F16))
        xc16 = E2(nc.sbuf_tensor("xc16", [DS, TT], F16))
        xdt = E2(nc.sbuf_tensor("xdt", [DR, TT], BF16))
        bbc = E2(nc.sbuf_tensor("bbc", [128, DS * TT], F16))
        cbc = E2(nc.sbuf_tensor("cbc", [128, DS * TT], F16))
        dab = E2(nc.sbuf_tensor("dab", [128, DS * BL], F16))
        dab2 = dab if dbg else E2(nc.sbuf_tensor("dab2", [128, DS * BL], F16))
        dbb = E2(nc.sbuf_tensor("dbb", [128, DS * BL], F16))
        hbb = E2(nc.sbuf_tensor("hbb", [128, DS * BL], F16))
        hbb2 = E2(nc.sbuf_tensor("hbb2", [128, DS * BL], F16))
        hcb = E2(nc.sbuf_tensor("hcb", [128, DS * TT], F16))
        ys16 = E2(nc.sbuf_tensor("ys16", [128, TT], F16))
        ysum = E2(nc.sbuf_tensor("ysum", [128, TT], F16))
        yt1 = E2(nc.sbuf_tensor("yt1", [128, TT], F16))
        yfin = E2(nc.sbuf_tensor("yfin", [128, NDT * TT], BF16))
        x1tm = E2(nc.sbuf_tensor("x1tm", [128, DM], F32))
        xres = E2(nc.sbuf_tensor("xres", [128, DM], F32))
        o2bfb = E2(nc.sbuf_tensor("o2bfb", [128, DM], BF16))
        o2fmb = o2bfb

        pA = E(nc.psum_tensor("pA", [128, 512], F32))
        pY = E(nc.psum_tensor("pY", [128, 512], F32))
        pO1 = E(nc.psum_tensor("pO1", [128, 512], F32))
        pO2 = E(nc.psum_tensor("pO2", [128, 512], F32))
        pO3 = E(nc.psum_tensor("pO3", [128, 512], F32))
        pO4 = E(nc.psum_tensor("pO4", [128, 512], F32))
        pT = E(nc.psum_tensor("pT", [128, 1024], BF16))

        def W5(w, k):
            return wbig[:, (w * NMT + k) * DM:(w * NMT + k + 1) * DM]

        for tok in [
            b.load(mcol[:], mcols[:, :]),
            b.load(convw[:], convw_c[:, :]),
            b.load(convb[:], convb_c[:, :]),

            b.load(dtw[:], dt_wTi[:, :]),
            b.load(dtb[:], dtb_c[:, :]),
            b.load(asb[:], alog_r[:, :]),
            b.load(dcol[:], d_c[:, :]),
            b.load(nw1[:], nw1_c[:, :]),
            b.load(nw2[:], nw2_c[:, :]),
            b.load(inbc[:], inb_c[:, :]),
            b.load(inbr[:], inb_r32[:, :]),
            b.load(outb2[:], outb32[:, :]),
            b.load(fbr2[:], fb32[:, :]),
            b.load(ones3[:], ones32[:, :]),
            b.load(ident[:], ident_in[:, :]),
            b.load(ones1[:], ones_row[:, :]),
            b.load(dk[:], decay[:, :]),
            b.load(dTt[:], diffsT[:, :]),
            b.load(cTt[:], causT[:, :]),
            b.load(gin_row[:], p_row[:, :]),
            b.load(cpc[:], cp_col[:, :]),
        ]:
            b.want(tok)

        b.sc(lambda s: s.activation(out=asb[:], in_=asb[:], func=AF.Exp))
        b.ve(lambda v: v.tensor_scalar(out=asb[:], in0=asb[:], scalar1=-1.0,
                                       scalar2=None, op0=ALU.mult))
        b.sc(lambda s: s.activation(out=gam[:], in_=dk[:], func=AF.Sigmoid))
        b.sc(lambda s: s.activation(out=logg1[:], in_=gam[:], func=AF.Ln))
        b.ve(lambda v: v.tensor_copy(loggb[:], logg1[:]))
        b.te(lambda t: t.matmul(pA[:, 0:1], ones1[0:1, :], loggb[0:1, :],
                                start=True, stop=True))
        b.ve(lambda v: v.tensor_copy(loggc[:], pA[:, 0:1]))
        b.sc(lambda s: s.activation(out=gin_row[:], in_=gin_row[:], func=AF.Exp,
                                    scale=logg1[:]))
        b.ve(lambda v: v.tensor_copy(gin_rowb[:], gin_row[:]))
        b.te(lambda t: t.matmul(pA[:, 0:MAC], ones1[0:1, :], gin_rowb[0:1, :],
                                start=True, stop=True))
        b.ve(lambda v: v.tensor_copy(gin_bc[:], pA[:, 0:MAC]))
        b.ve(lambda v: v.tensor_copy(ginr32[:], gin_rowb[:]))
        b.ve(lambda v: v.tensor_tensor(out=gin_row2[:], in0=gin_row[:], in1=ginr32[:],
                                       op=ALU.subtract))
        b.ve(lambda v: v.tensor_copy(gin_rowb[:], gin_row2[:]))
        b.te(lambda t: t.matmul(pA[:, 0:MAC], ones1[0:1, :], gin_rowb[0:1, :],
                                start=True, stop=True))
        b.ve(lambda v: v.tensor_copy(gin_lo[:], pA[:, 0:MAC]))
        b.sc(lambda s: s.activation(out=gw_col[:], in_=cpc[:], func=AF.Exp,
                                    scale=loggc[0:MAC, :]))
        b.ve(lambda v: v.memset(c128[:], float(MAC)))
        b.sc(lambda s: s.activation(out=gc_col[:], in_=c128[:], func=AF.Exp,
                                    scale=loggc[:]))
        b.sc(lambda s: s.activation(out=dTt[:], in_=dTt[:], func=AF.Exp,
                                    scale=loggc[0:MAC, :]))
        b.ve(lambda v: v.tensor_tensor(out=mTt[:], in0=dTt[:], in1=cTt[:], op=ALU.mult))

        for k in range(NDT):
            tok = b.load(xtm[:, 0:128], xp_pad[k * 128:(k + 1) * 128, :])
            b.want(tok)
            b.ve(lambda v, k=k: v.tensor_copy(xpw[:, k * 128:(k + 1) * 128],
                                              xtm[:, 0:128]))
            b.ve(lambda v, k=k: v.tensor_copy(jsq[:, 0:128],
                                              xpw[:, k * 128:(k + 1) * 128]))
            b.ve(lambda v, k=k: v.tensor_tensor(out=xpl[:, k * 128:(k + 1) * 128],
                                                in0=xtm[:, 0:128], in1=jsq[:, 0:128],
                                                op=ALU.subtract))

        def init_cols(v):
            for buf in ([dab, dab2] if dab2 is not dab else [dab]):
                v.memset(buf[:].rearrange("p (s c) -> p s c", c=BL)[:, :, 0:1],
                         0.0)
                v.memset(buf[:].rearrange("p (s c) -> p s c", c=BL)[:, :, 1:2],
                         1.0)
            v.memset(dbb[:].rearrange("p (s c) -> p s c", c=BL)[:, :, 1:2], 0.0)
            v.memset(zcol[:], 0.0)
            v.memset(epscol[:], EPS)
            v.memset(onecol[:], 1.0)
            v.memset(ones2r[:], 1.0)
            return v.memset(zerocol[:], 0.0)
        b.ve(init_cols)
        for mt in range(NMT):
            b.store(o2fm[mt * 128:(mt + 1) * 128, 0:1], zcol[:], slow=True)

        def dn(d):
            return "del32a" if d < NDT // 2 else "del32b"

        def emit_call(call):
            src_t = xin if call == 1 else x1sp
            nw = nw1 if call == 1 else nw2
            PB = (pA, pY, pO1, pO2, pO3, pO4)
            PBN = ("pb0", "pb1", "pb2", "pb3", "pb4", "pb5")
            DL = ["dl%d" % d for d in range(NDT)]
            SL = ["sl%d" % d for d in range(NDT)]

            def reset(v):
                v.memset(xzc[:].rearrange("p (d c) -> p d c", c=CBL)[:, :, 0:4],
                         0.0)
                return v.memset(carry[:], 0.0)
            b.ve(reset)

            def produce_stages(it):
                t0 = it * TT
                mt = mtile[:, (it % 2) * TT:(it % 2) * TT + TT]
                mtn = "mt%d" % (it % 2)
                st = []

                def G(g, cls):
                    return g if cls in PUMP_CLASSES else NDT
                st.append((G(0, 'n1'), lambda: b.load(
                    mt, mrow_16[0:1, t0:t0 + TT].partition_broadcast(128),
                    q=b.qg, w=[mtn])))
                xsrc = [] if call == 1 else ["d:x1sp"]
                for tc in range(TCH):
                    st.append((G(0, 'n1'), lambda tc=tc: b.load(
                        xtm[:], src_t[t0 + tc * 128:t0 + (tc + 1) * 128, :],
                        q=b.qs, r=xsrc, w=["xtm"])))
                    st.append((G(0, 'n1'), lambda: b.sc(
                        lambda s: s.activation(out=jsq[:], in_=xtm[:],
                                               func=AF.Square),
                        r=["xtm"], w=["jsq"])))
                    st.append((G(0, 'n1'), lambda: b.ve(
                        lambda v: v.reduce_sum(col1[:], jsq[:],
                                               axis=mybir.AxisListType.X),
                        r=["jsq"], w=["c1"])))
                    st.append((G(0, 'n1'), lambda: b.sc(
                        lambda s: s.activation(out=col2[:], in_=col1[:],
                                               func=AF.Sqrt, scale=1.0 / DM,
                                               bias=epscol[:]),
                        r=["c1"], w=["c2"])))
                    st.append((G(0, 'n1'), lambda: b.ve(
                        lambda v: v.reciprocal(out=col3[:], in_=col2[:]),
                        r=["c2"], w=["c3"])))
                    st.append((G(0, 'n1'), lambda: b.sc(
                        lambda sc: sc.activation(out=o2bfb[:], in_=xtm[:],
                                                 func=AF.Copy, scale=col3[:]),
                        r=["xtm", "c3"], w=["o2bfb"])))

                    def transp2(t):
                        last = None
                        for k in range(NMT):
                            last = t.transpose(pT[:, k * 128:(k + 1) * 128],
                                               o2bfb[:, k * 128:(k + 1) * 128],
                                               ident[:])
                        return last
                    st.append((G(0, 'n2'), lambda transp2=transp2: b.te(
                        transp2, r=["o2bfb"], w=["pT"])))

                    def evtr(v, tc=tc):
                        last = None
                        for k in range(NMT):
                            last = v.tensor_scalar(
                                out=xrfm[:, k * TT + tc * 128:k * TT + (tc + 1) * 128],
                                in0=pT[:, k * 128:(k + 1) * 128],
                                scalar1=nw[:, k:k + 1], scalar2=None,
                                op0=ALU.mult)
                        return last
                    st.append((G(0, 'n2'), lambda evtr=evtr: b.ve(
                        evtr, r=["pT"], w=["xrfm"])))

                pos = 0
                first = {"v": True}

                def wload(oc, pos):
                    stage = wstage if pos % 2 == 0 else wstage2
                    wsn = "ws0" if pos % 2 == 0 else "ws1"
                    return (G(0, 'w'), lambda: b.load(
                        stage[:].rearrange("p (k m) -> p k m", m=128),
                        in_wT[:, oc * 128:(oc + 1) * 128]
                        .rearrange("(k p) m -> p k m", p=128), w=[wsn]))

                def mm_st(oc, pos, gate, bias=False):
                    stage = wstage if pos % 2 == 0 else wstage2
                    wsn = "ws0" if pos % 2 == 0 else "ws1"
                    pacc = PB[pos % 2]
                    pn = PBN[pos % 2]

                    def mm(t, oc=oc, stage=stage, pacc=pacc, bias=bias):
                        if bias:
                            t.matmul(pacc[:, 0:TT],
                                     inbr[0:1, oc * 128:(oc + 1) * 128],
                                     ones2r[0:1, 0:TT], start=True, stop=False)
                        last = None
                        for k in range(NMT):
                            last = t.matmul(pacc[:, 0:TT],
                                            stage[:, k * 128:(k + 1) * 128],
                                            xrfm[:, k * TT:(k + 1) * TT],
                                            start=(not bias and k == 0),
                                            stop=(k == NMT - 1))
                        return last
                    return (G(gate, 'm'), lambda mm=mm: b.te(mm, r=[wsn, "xrfm"],
                                                     w=[pn]))

                st.append(wload(0, 0))
                for j in range(NDT):
                    oc1, oc2 = j, j + NDT
                    p1, p2 = 2 * j, 2 * j + 1
                    st.append(mm_st(oc1, p1, max(0, j - 1)))
                    st.append(wload(oc2, p2))

                    def ev1(v, d=j, oc=oc1, mt=mt, pacc=PB[p1 % 2]):
                        return v.scalar_tensor_tensor(
                            out=xzc[:, d * CBL + 4:d * CBL + 4 + TT],
                            in0=pacc[:, 0:TT], scalar=inbc[:, oc:oc + 1],
                            in1=mt, op0=ALU.add, op1=ALU.mult)
                    st.append((G(max(0, j - 1), 'e1'), lambda ev1=ev1, pn=PBN[p1 % 2]:
                               b.ve(ev1, r=[pn, mtn], w=["xzc"])))
                    st.append(mm_st(oc2, p2, max(0, j - 1), bias=True))
                    if j + 1 < NDT:
                        st.append(wload(oc1 + 1, p2 + 1))

                    def ev2(s, d=j, pacc=PB[p2 % 2]):
                        return s.activation(out=silu_r[:, d * TT:(d + 1) * TT],
                                            in_=pacc[:, 0:TT], func=AF.Copy)
                    st.append((G(j + 1, 'e2'), lambda ev2=ev2, pn=PBN[p2 % 2], d=j:
                               b.sc(ev2, r=[pn], w=[SL[d]])))

                    base = j * CBL

                    def cv0(v, base=base, d=j):
                        return v.tensor_scalar(
                            out=del32[:, d * TT:(d + 1) * TT],
                            in0=xzc[:, base + 1:base + 1 + TT],
                            scalar1=convw[:, d * 4:d * 4 + 1],
                            scalar2=convb[:, d:d + 1],
                            op0=ALU.mult, op1=ALU.add)
                    st.append((G(j, 'cv'), lambda cv0=cv0, d=j: b.ve(
                        cv0, r=["xzc"], w=[DL[d]])))
                    for k in range(1, 4):
                        def cvk(v, base=base, d=j, k=k):
                            return v.scalar_tensor_tensor(
                                out=del32[:, d * TT:(d + 1) * TT],
                                in0=xzc[:, base + 1 + k:base + 1 + k + TT],
                                scalar=convw[:, d * 4 + k:d * 4 + k + 1],
                                in1=del32[:, d * TT:(d + 1) * TT],
                                op0=ALU.mult, op1=ALU.add)
                        st.append((G(j, 'cv'), lambda cvk=cvk, d=j: b.ve(
                            cvk, r=["xzc", DL[d]], w=[DL[d]])))

                def hist(v):
                    return v.tensor_copy(
                        xzc[:].rearrange("p (d c) -> p d c", c=CBL)[:, :, 1:4],
                        xzc[:].rearrange("p (d c) -> p d c", c=CBL)
                        [:, :, 1 + TT:4 + TT])
                st.append((NDT, lambda: b.ve(hist, r=["xzc"], w=["xzc"])))
                return st

            def emit_mid(it):
                t0 = it * TT
                mt = mtile[:, (it % 2) * TT:(it % 2) * TT + TT]
                mtn = "mt%d" % (it % 2)
                b.sc(lambda s: s.activation(out=silu_r[:], in_=silu_r[:],
                                            func=AF.Silu, scale=onecol[:],
                                            bias=zerocol[:]),
                     r=SL, w=SL)
                b.sc(lambda s: s.activation(out=xm16[:], in_=del32[:],
                                            func=AF.Silu, scale=onecol[:],
                                            bias=zerocol[:]),
                     r=DL, w=["xm16"])

                def mk1(v, mt=mt):
                    return v.tensor_tensor(
                        out=hcb[:].rearrange("p (d t) -> p d t", t=TT),
                        in0=mt[:, None, :].broadcast_to([128, NDT, TT]),
                        in1=xm16[:].rearrange("p (d t) -> p d t", t=TT),
                        op=ALU.mult)
                b.ve(mk1, r=["xm16", mtn], w=["hcb"])
                b.ve(lambda v: v.tensor_copy(xmbf[:], hcb[:]),
                     r=["hcb"], w=["xmbf"])
                b.ve(lambda v: v.tensor_tensor(out=xmlo[:], in0=hcb[:],
                                               in1=xmbf[:], op=ALU.subtract),
                     r=["hcb", "xmbf"], w=["xmlo"])

                def xproj(t):
                    last = None
                    for k in range(NDT):
                        t.matmul(pA[:, 0:TT], xpw[:, k * 128:(k + 1) * 128],
                                 xmbf[:, k * TT:(k + 1) * TT],
                                 start=(k == 0), stop=False)
                        t.matmul(pA[:, 0:TT], xpw[:, k * 128:(k + 1) * 128],
                                 xmlo[:, k * TT:(k + 1) * TT],
                                 start=False, stop=False)
                        last = t.matmul(pA[:, 0:TT], xpl[:, k * 128:(k + 1) * 128],
                                        xmbf[:, k * TT:(k + 1) * TT],
                                        start=False, stop=(k == NDT - 1))
                    return last
                b.te(xproj, r=["xmbf", "xmlo"], w=["pb0"])

                def bcd_ev(sc):
                    sc.activation(out=xb16[:], in_=pA[0:DS, 0:TT], func=AF.Copy)
                    sc.activation(out=xc16[:], in_=pA[32:32 + DS, 0:TT],
                                  func=AF.Copy)
                    return sc.activation(out=xdt[:], in_=pA[64:128, 0:TT],
                                         func=AF.Copy)
                b.sc(bcd_ev, r=["pb0"], w=["xb16", "xc16", "xdt"])
                b.load(bsp[0, :, t0:t0 + TT], xb16[:], q=b.qg,
                       r=["xb16"], w=["d:bsp"])
                b.load(csp[0, :, t0:t0 + TT], xc16[:], q=b.qg,
                       r=["xc16"], w=["d:csp"])
                b.load(bbc[:].rearrange("p (s t) -> p s t", t=TT),
                       bsp[0:1, :, t0:t0 + TT].partition_broadcast(128), q=b.qg,
                       r=["d:bsp"], w=["bbc"])
                b.load(cbc[:].rearrange("p (s t) -> p s t", t=TT),
                       csp[0:1, :, t0:t0 + TT].partition_broadcast(128), q=b.qg,
                       r=["d:csp"], w=["cbc"])

                for w0 in range(0, NDT, 6):
                    wave = list(range(w0, min(w0 + 6, NDT)))
                    for d in wave:
                        def dtmm(t, d=d, pacc=PB[d % 6]):
                            return t.matmul(pacc[:, 0:TT],
                                            dtw[:, d * 128:(d + 1) * 128],
                                            xdt[:, 0:TT], start=True, stop=True)
                        b.te(dtmm, r=["xdt"], w=[PBN[d % 6]])
                    for j, d in enumerate(wave):
                        rdep = [PBN[dd % 6] for dd in wave] if j == 0 else [PBN[d % 6]]

                        def sp1(s, d=d, pacc=PB[d % 6]):
                            return s.activation(out=del32[:, d * TT:(d + 1) * TT],
                                                in_=pacc[:, 0:TT], func=AF.Exp,
                                                scale=onecol[:],
                                                bias=dtb[:, d:d + 1])
                        b.sc(sp1, r=rdep, w=[DL[d]])
                    if w0 + 6 >= NDT // 2 and w0 < NDT // 2:
                        hh = NDT // 2 * TT
                        b.sc(lambda s: s.activation(
                            out=del32[:, 0:hh], in_=del32[:, 0:hh],
                            func=AF.Ln, scale=onecol[:], bias=onecol[:]),
                             r=DL[:NDT // 2], w=DL[:NDT // 2])
                hh = NDT // 2 * TT
                b.sc(lambda s: s.activation(
                    out=del32[:, hh:], in_=del32[:, hh:],
                    func=AF.Ln, scale=onecol[:], bias=onecol[:]),
                     r=DL[NDT // 2:], w=DL[NDT // 2:])

            def emit_dloop(it, pump):
                mt = mtile[:, (it % 2) * TT:(it % 2) * TT + TT]
                mtn = "mt%d" % (it % 2)
                pidx = [0]

                def drain(limit):
                    while pidx[0] < len(pump) and pump[pidx[0]][0] <= limit:
                        pump[pidx[0]][1]()
                        pidx[0] += 1

                def emit_tail(d):
                    def ts3(v):
                        return v.tensor_tensor(
                            out=hcb[:, 0:2 * TT].rearrange("p (s t) -> p s t", t=TT),
                            in0=hcb[:, 0:4 * TT]
                            .rearrange("p (s t) -> p s t", t=2 * TT)[:, :, 0:TT],
                            in1=hcb[:, 0:4 * TT]
                            .rearrange("p (s t) -> p s t", t=2 * TT)[:, :, TT:2 * TT],
                            op=ALU.add)
                    b.ve(ts3, r=["hcb"], w=["hcb"])
                    b.ve(lambda v: v.tensor_tensor(
                        out=ysum[:], in0=hcb[:, 0:TT], in1=hcb[:, TT:2 * TT],
                        op=ALU.add),
                         r=["hcb"], w=["ysum"])
                    b.ve(lambda v, d=d: v.scalar_tensor_tensor(
                        out=yt1[:], in0=xm16[:, d * TT:(d + 1) * TT],
                        scalar=dcol[:, d:d + 1], in1=ysum[:],
                        op0=ALU.mult, op1=ALU.add),
                         r=["xm16", "ysum"], w=["yt1"])
                    b.ve(lambda v, d=d: v.tensor_tensor(
                        out=yfin[:, d * TT:(d + 1) * TT], in0=yt1[:],
                        in1=silu_r[:, d * TT:(d + 1) * TT], op=ALU.mult),
                         r=["yt1", SL[d]], w=["yf%d" % d])
                    stage = ostage if d % 2 == 0 else ostage2
                    osn = "os0" if d % 2 == 0 else "os1"
                    nxt = ostage2 if d % 2 == 0 else ostage
                    nxo = "os1" if d % 2 == 0 else "os0"
                    if d + 1 < NDT:
                        b.load(nxt[:], out_wTi[(d + 1) * 128:(d + 2) * 128, :],
                               q=b.qg, w=[nxo])

                    def omm(t, k=d, stage=stage):
                        last = None
                        for tc in range(TCH):
                            for ng in range(2):
                                pD = (pO1, pO2, pO3, pO4)[tc * 2 + ng]
                                last = t.matmul(
                                    pD[:, 0:512],
                                    yfin[:, k * TT + tc * 128:k * TT + (tc + 1) * 128],
                                    stage[:, ng * 512:(ng + 1) * 512],
                                    start=(k == 0), stop=False)
                        return last

                    def omm_stage(omm=omm, d=d, osn=osn):
                        b.te(omm, r=[osn, "yf%d" % d] +
                             (["pb2", "pb3", "pb4", "pb5"] if d else []),
                             w=["pb2", "pb3", "pb4", "pb5"])
                    if OMM_TAIL:
                        omm_stage()
                    else:
                        deferred_omm.append(omm_stage)

                prev = None
                deferred_omm = []
                b.load(ostage[:], out_wTi[0:128, :], q=b.qg, w=["os0"])
                for d in range(NDT):
                    dabX, dnm = (dab, "dab0") if d % 2 == 0 else (dab2, "dab1")
                    hX, hn = (hbb, "hbb0") if d % 2 == 0 else (hbb2, "hbb1")

                    def dact(s, d=d, dabX=dabX):
                        last = None
                        for s_ in range(DS):
                            last = s.activation(
                                out=dabX[:, s_ * BL + 2:s_ * BL + 2 + TT],
                                in_=del32[:, d * TT:(d + 1) * TT],
                                func=AF.Exp,
                                scale=asb[:, d * DS + s_:d * DS + s_ + 1],
                                bias=zerocol[:])
                        return last
                    b.sc(dact, r=[DL[d]], w=[dnm])

                    b.ve(lambda v, d=d, mt=mt: v.tensor_tensor(
                        out=dtmp[:], in0=del32[:, d * TT:(d + 1) * TT],
                        in1=mt, op=ALU.mult),
                         r=[DL[d], mtn], w=["dtmp"])
                    b.ve(lambda v, d=d: v.tensor_tensor(
                        out=ys16[:], in0=dtmp[:],
                        in1=xm16[:, d * TT:(d + 1) * TT], op=ALU.mult),
                         r=["dtmp", "xm16"], w=["ys16"])

                    def dbu(v, d=d):
                        v.tensor_tensor(
                            out=dbb[:].rearrange("p (s c) -> p s c", c=BL)
                            [:, :, 2:2 + TT],
                            in0=ys16[:, None, :].broadcast_to([128, DS, TT]),
                            in1=bbc[:].rearrange("p (s t) -> p s t", t=TT),
                            op=ALU.mult)
                        return v.tensor_copy(
                            dbb[:].rearrange("p (s c) -> p s c", c=BL)[:, :, 0:1],
                            carry[:, d * DS:(d + 1) * DS]
                            .rearrange("p (s c) -> p s c", c=1))
                    b.ve(dbu, r=["ys16", "bbc", "carry"], w=["dbb"])

                    b.ve(lambda v, dabX=dabX, hX=hX: v.tensor_tensor_scan(
                        out=hX[:], data0=dabX[:], data1=dbb[:], initial=0.0,
                        op0=ALU.mult, op1=ALU.add),
                         r=[dnm, "dbb"], w=[hn])

                    def carr(v, d=d, hX=hX):
                        return v.tensor_copy(
                            carry[:, d * DS:(d + 1) * DS].rearrange("p (s c) -> p s c", c=1),
                            hX[:].rearrange("p (s c) -> p s c", c=BL)[:, :, BL - 1:BL])
                    b.ve(carr, r=[hn], w=["carry"])

                    if prev is not None:
                        emit_tail(prev)

                    def cmg(g, hX=hX):
                        return g.tensor_tensor(
                            out=hcb[:].rearrange("p (s t) -> p s t", t=TT),
                            in0=hX[:].rearrange("p (s c) -> p s c", c=BL)
                            [:, :, 2:2 + TT],
                            in1=cbc[:].rearrange("p (s t) -> p s t", t=TT),
                            op=ALU.mult)
                    b.ve(cmg, r=[hn, "cbc"], w=["hcb"])

                    def ts1(g):
                        return g.tensor_tensor(
                            out=hcb[:, 0:8 * TT].rearrange("p (s t) -> p s t", t=TT),
                            in0=hcb[:].rearrange("p (s t) -> p s t", t=2 * TT)
                            [:, :, 0:TT],
                            in1=hcb[:].rearrange("p (s t) -> p s t", t=2 * TT)
                            [:, :, TT:2 * TT],
                            op=ALU.add)
                    b.ve(ts1, r=["hcb"], w=["hcb"])

                    def ts2(g):
                        return g.tensor_tensor(
                            out=hcb[:, 0:4 * TT].rearrange("p (s t) -> p s t", t=TT),
                            in0=hcb[:, 0:8 * TT]
                            .rearrange("p (s t) -> p s t", t=2 * TT)[:, :, 0:TT],
                            in1=hcb[:, 0:8 * TT]
                            .rearrange("p (s t) -> p s t", t=2 * TT)[:, :, TT:2 * TT],
                            op=ALU.add)
                    b.ve(ts2, r=["hcb"], w=["hcb"])
                    if PUMP:
                        drain(d)
                    prev = d
                emit_tail(prev)
                for fn in deferred_omm:
                    fn()
                drain(NDT)

            def emit_merge(it):
                t0 = it * TT

                def obias(t):
                    last = None
                    for tc in range(TCH):
                        for ng in range(2):
                            pD = (pO1, pO2, pO3, pO4)[tc * 2 + ng]
                            last = t.matmul(pD[:, 0:512], ones3[0:1, :],
                                            outb2[0:1, ng * 512:(ng + 1) * 512],
                                            start=False, stop=True)
                    return last
                b.te(obias, r=["pb2", "pb3", "pb4", "pb5"],
                     w=["pb2", "pb3", "pb4", "pb5"])

                for tc in range(TCH):
                    pL = (pO1, pO3)[tc]
                    pR = (pO2, pO4)[tc]
                    if call == 1:
                        b.load(xres[:], src_t[t0 + tc * 128:t0 + (tc + 1) * 128, :],
                               q=b.qs, w=["xres"])

                        def merge(v, pL=pL, pR=pR):
                            v.tensor_tensor(out=x1tm[:, 0:512], in0=xres[:, 0:512],
                                            in1=pL[:, 0:512], op=ALU.add)
                            return v.tensor_tensor(out=x1tm[:, 512:1024],
                                                   in0=xres[:, 512:1024],
                                                   in1=pR[:, 0:512], op=ALU.add)
                        b.ve(merge, r=["pb2", "pb3", "pb4", "pb5", "xres"],
                             w=["x1tm"])
                        b.store(x1sp[t0 + tc * 128:t0 + (tc + 1) * 128, :], x1tm[:],
                                r=["x1tm"], w=["d:x1sp"])
                    else:
                        def merge2(sc, pL=pL, pR=pR):
                            sc.activation(out=x1tm[:, 0:512], in_=pL[:, 0:512],
                                          func=AF.Copy)
                            return sc.activation(out=x1tm[:, 512:1024],
                                                 in_=pR[:, 0:512], func=AF.Copy)
                        b.sc(merge2, r=["pb2", "pb3", "pb4", "pb5"], w=["x1tm"])
                        b.store(o2tm[t0 + tc * 128:t0 + (tc + 1) * 128, :], x1tm[:],
                                r=["x1tm"], w=["d:o2tm"])
                        cix = (t0 + tc * 128) // 128
                        b.sc(lambda sc, cix=cix: sc.activation(
                            out=o2bfb[:], in_=x1tm[:], func=AF.Copy,
                            scale=mcol[:, cix:cix + 1]),
                             r=["x1tm"], w=["o2bfb"])

                        def tr2(t):
                            last = None
                            for k in range(NMT):
                                last = t.transpose(pT[:, k * 128:(k + 1) * 128],
                                                   o2bfb[:, k * 128:(k + 1) * 128],
                                                   ident[:])
                            return last
                        b.te(tr2, r=["o2bfb"], w=["pT"])

                        b.sc(lambda sc: sc.activation(out=o2fmb[:], in_=pT[:, 0:DM],
                                                      func=AF.Copy),
                             r=["pT"], w=["o2bfb"])
                        b.store(o2fm[:, 1 + t0 + tc * 128:1 + t0 + (tc + 1) * 128]
                                .rearrange("(k p) m -> p k m", p=128),
                                o2fmb[:].rearrange("p (k m) -> p k m", m=128),
                                r=["o2bfb"], w=["d:o2fm"])

            for fn in produce_stages(0):
                fn[1]()
            for it in range(NT):
                emit_mid(it)
                nxt_stages = produce_stages(it + 1) if it + 1 < NT else []
                emit_dloop(it, nxt_stages)
                emit_merge(it)

        emit_call(1)
        emit_call(2)
        es2.close()

        wbig = E(nc.sbuf_tensor("wbig", [128, 5 * NMT * DM], BF16))
        w32stg = E(nc.sbuf_tensor("w32stg", [128, DM], F32))
        hi32stg = E(nc.sbuf_tensor("hi32stg", [128, DM], F32))
        lost1 = E(nc.sbuf_tensor("lost1", [128, DM], BF16))
        lost2 = E(nc.sbuf_tensor("lost2", [128, DM], BF16))
        xalo = E(nc.sbuf_tensor("xalo", [128, DM], BF16))
        h2lo = E(nc.sbuf_tensor("h2lo", [128, DM], BF16))

        WT32 = E(nc.sbuf_tensor("WT32", [128, NMT * DM], F32))
        WTbf = E(nc.sbuf_tensor("WTbf", [128, NMT * DM], BF16))
        rkb = E(nc.sbuf_tensor("rkb", [128, NMT * MAC], BF16))
        wkb = E(nc.sbuf_tensor("wkb", [128, NMT * MAC], BF16))
        wktm = E(nc.sbuf_tensor("wktm", [128, DM], BF16))
        vbf = E(nc.sbuf_tensor("vbf", [128, DM], BF16))
        vgw = E(nc.sbuf_tensor("vgw", [128, DM], BF16))
        stp = E(nc.sbuf_tensor("stp", [MAC, MAC], F32))
        smt = E(nc.sbuf_tensor("smt", [MAC, MAC], BF16))
        rkg = E(nc.sbuf_tensor("rkg", [128, NMT * MAC], BF16))
        rkg_lo = xalo
        rdsb = E(nc.sbuf_tensor("rdsb", [128, NMT * MAC], BF16))
        attn = E(nc.sbuf_tensor("attn", [128, DM], F32))
        xa32 = E(nc.sbuf_tensor("xa32", [128, DM], F32))


        b.ve(lambda v: v.memset(WT32[:], 0.0))
        b.ve(lambda v: v.memset(WTbf[:], 0.0))
        for w, ap in enumerate([wW, wR, fW1, fW2, fW3]):
            for k in range(NMT):
                b.load(w32stg[:], ap[k * 128:(k + 1) * 128, :], w=["w32stg"])
                b.ve(lambda v, w=w, k=k: v.tensor_copy(W5(w, k), w32stg[:]),
                     r=["w32stg"], w=["wbig"])
                b.ve(lambda v, w=w, k=k: v.tensor_copy(hi32stg[:], W5(w, k)),
                     r=["wbig"], w=["hi32"])
                b.ve(lambda v: v.tensor_tensor(out=lost1[:], in0=w32stg[:],
                                               in1=hi32stg[:], op=ALU.subtract),
                     r=["w32stg", "hi32"], w=["lost1"])
                b.store(wlo[w, k * 128:(k + 1) * 128, :], lost1[:],
                        r=["lost1"], w=["d:wlo"])
        reads32 = attn
        x1ch = hi32stg
        o2ch = w32stg
        xch = jsq
        for c in range(NCH):
            ct0 = c * MAC
            b.load(rkb[:].rearrange("p (k m) -> p k m", m=MAC),
                   o2fm[:, 1 + ct0:1 + ct0 + MAC]
                   .rearrange("(k p) m -> p k m", p=128),
                   r=["d:o2fm"], w=["rkb"])
            b.load(wkb[:].rearrange("p (k m) -> p k m", m=MAC),
                   o2fm[:, ct0:ct0 + MAC].rearrange("(k p) m -> p k m", p=128),
                   r=["d:o2fm"], w=["wkb"])

            def vmm(t):
                last = None
                for ng in range(2):
                    pD = (pO1, pO2)[ng]
                    for k in range(NMT):
                        last = t.matmul(pD[:, 0:512], rkb[:, k * MAC:(k + 1) * MAC],
                                        W5(0, k)[:, ng * 512:(ng + 1) * 512],
                                        start=(k == 0), stop=False)
                return last
            b.te(vmm, r=["rkb", "wbig"], w=["pO12"])
            b.load(lost1[:], wlo[0, 0:128, :], r=["d:wlo"], w=["lost1"])
            for k in range(NMT):
                stg = lost1 if k % 2 == 0 else lost2
                stn = "lost1" if k % 2 == 0 else "lost2"
                nxtstg = lost2 if k % 2 == 0 else lost1
                nxtn = "lost2" if k % 2 == 0 else "lost1"
                if k + 1 < NMT:
                    b.load(nxtstg[:], wlo[0, (k + 1) * 128:(k + 2) * 128, :],
                           r=["d:wlo"], w=[nxtn])

                def vlo(t, k=k, stg=stg):
                    last = None
                    for ng in range(2):
                        pD = (pO1, pO2)[ng]
                        last = t.matmul(pD[:, 0:512], rkb[:, k * MAC:(k + 1) * MAC],
                                        stg[:, ng * 512:(ng + 1) * 512],
                                        start=False, stop=(k == NMT - 1))
                    return last
                b.te(vlo, r=["rkb", stn, "pO12"], w=["pO12"])
            b.ve(lambda v: v.tensor_tensor(out=gwm[:], in0=gw_col[:],
                                           in1=mcol[:, c:c + 1], op=ALU.mult),
                 w=["gwm"], r=[])

            def vev(v):
                v.tensor_copy(vbf[:, 0:512], pO1[:, 0:512])
                v.tensor_copy(vbf[:, 512:1024], pO2[:, 0:512])
                v.tensor_scalar(out=vgw[:, 0:512], in0=pO1[:, 0:512],
                                scalar1=gwm[:], scalar2=None, op0=ALU.mult)
                return v.tensor_scalar(out=vgw[:, 512:1024], in0=pO2[:, 0:512],
                                       scalar1=gwm[:], scalar2=None, op0=ALU.mult)
            b.ve(vev, r=["pO12", "gwm"], w=["vbf", "vgw"])

            def wtr(t):
                last = None
                for mt in range(NMT):
                    last = t.transpose(pT[:, mt * 128:(mt + 1) * 128],
                                       wkb[:, mt * MAC:(mt + 1) * MAC], ident[:])
                return last
            b.te(wtr, r=["wkb"], w=["pT"])
            b.ve(lambda v: v.tensor_copy(wktm[:], pT[:, 0:DM]),
                 r=["pT"], w=["wktm"])

            if c >= WCH0:
                def stmm(t):
                    last = None
                    for k in range(NMT):
                        last = t.matmul(pA[:, 0:MAC], wkb[:, k * MAC:(k + 1) * MAC],
                                        rkb[:, k * MAC:(k + 1) * MAC],
                                        start=(k == 0), stop=(k == NMT - 1))
                    return last
                b.te(stmm, r=["wkb", "rkb"], w=["pA"])
                b.ve(lambda v: v.tensor_copy(stp[:], pA[:, 0:MAC]),
                     r=["pA"], w=["stp"])
                b.ve(lambda v: v.tensor_tensor(out=smt[:], in0=stp[:], in1=mTt[:],
                                               op=ALU.mult),
                     r=["stp"], w=["smt"])
                def rg(v):
                    last = None
                    for k in range(NMT):
                        last = v.tensor_tensor(out=rkg[:, k * MAC:(k + 1) * MAC],
                                               in0=rkb[:, k * MAC:(k + 1) * MAC],
                                               in1=gin_bc[:], op=ALU.mult)
                    return last
                b.ve(rg, r=["rkb"], w=["rkg"])

                def rg2(v):
                    last = None
                    for k in range(NMT):
                        last = v.tensor_tensor(out=rkg_lo[:, k * MAC:(k + 1) * MAC],
                                               in0=rkb[:, k * MAC:(k + 1) * MAC],
                                               in1=gin_lo[:], op=ALU.mult)
                    return last
                b.ve(rg2, r=["rkb"], w=["xalo"])

                def ii(t):
                    last = None
                    for dt_ in range(NMT):
                        pD = (pO3, pO4)[dt_ // 4]
                        sl = slice((dt_ % 4) * 128, (dt_ % 4) * 128 + 128)
                        for et in range(NMT):
                            t.matmul(pD[:, sl],
                                     WTbf[:, (et * NMT + dt_) * 128:
                                          (et * NMT + dt_) * 128 + 128],
                                     rkg[:, et * MAC:(et + 1) * MAC],
                                     start=(et == 0), stop=False)
                            t.matmul(pD[:, sl],
                                     WTbf[:, (et * NMT + dt_) * 128:
                                          (et * NMT + dt_) * 128 + 128],
                                     rkg_lo[:, et * MAC:(et + 1) * MAC],
                                     start=False, stop=False)
                        last = t.matmul(pD[:, sl], vbf[:, dt_ * 128:(dt_ + 1) * 128],
                                        smt[:], start=False, stop=True)
                    return last
                b.te(ii, r=["WTbf", "rkg", "xalo", "vbf", "smt"], w=["pO34"])

                def rev(v):
                    v.tensor_copy(reads32[:, 0:512], pO3[:, 0:512])
                    return v.tensor_copy(reads32[:, 512:1024], pO4[:, 0:512])
                b.ve(rev, r=["pO34"], w=["attn"])
                b.ve(lambda v: v.tensor_copy(rdsb[:], reads32[:]),
                     r=["attn"], w=["rdsb"])

            for et in range(NMT):
                pU1 = pO1 if et % 2 == 0 else pO3
                pU2 = pO2 if et % 2 == 0 else pO4
                pun = "pO12" if et % 2 == 0 else "pO34"

                def wmm(t, et=et, pU1=pU1, pU2=pU2):
                    t.matmul(pU1[:, 0:512], wktm[:, et * 128:(et + 1) * 128],
                             vgw[:, 0:512], start=True, stop=True)
                    return t.matmul(pU2[:, 0:512], wktm[:, et * 128:(et + 1) * 128],
                                    vgw[:, 512:1024], start=True, stop=True)
                b.te(wmm, r=["wktm", "vgw"], w=[pun])

                def wupd(v, et=et, pU1=pU1, pU2=pU2):
                    base = et * DM
                    v.scalar_tensor_tensor(out=WT32[:, base:base + 512],
                                           in0=WT32[:, base:base + 512],
                                           scalar=gc_col[:], in1=pU1[:, 0:512],
                                           op0=ALU.mult, op1=ALU.add)
                    return v.scalar_tensor_tensor(out=WT32[:, base + 512:base + 1024],
                                                  in0=WT32[:, base + 512:base + 1024],
                                                  scalar=gc_col[:], in1=pU2[:, 0:512],
                                                  op0=ALU.mult, op1=ALU.add)
                b.ve(wupd, r=[pun, "WT32"], w=["WT32"])
            b.sc(lambda sc: sc.activation(out=WTbf[:], in_=WT32[:], func=AF.Copy),
                 r=["WT32"], w=["WTbf"])

            if c < WCH0:
                continue

            if c == WCH0 and dbg:
                DUMP("reads32", reads32[:])
                DUMP("rkb", rkb[:])
                DUMP("vbf", vbf[:])
                DUMP("smt", smt[:])
                DUMP("rdsb", rdsb[:])
                DUMP("WTbf", WTbf[:, 0:4096])
            def rpmm(t):
                last = None
                for ng in range(2):
                    pD = (pO1, pO2)[ng]
                    for k in range(NMT):
                        last = t.matmul(pD[:, 0:512], rdsb[:, k * MAC:(k + 1) * MAC],
                                        W5(1, k)[:, ng * 512:(ng + 1) * 512],
                                        start=(k == 0), stop=(k == NMT - 1))
                return last
            b.te(rpmm, r=["rdsb", "wbig"], w=["pO12"])
            b.load(xch[:], xin[ct0:ct0 + MAC, :], r=[], w=["jsq"])
            b.load(x1ch[:], x1sp[ct0:ct0 + MAC, :], r=["d:x1sp"], w=["hi32"])
            b.load(o2ch[:], o2tm[ct0:ct0 + MAC, :], r=["d:o2tm"], w=["w32stg"])

            def mg(v):
                v.scalar_tensor_tensor(out=attn[:, 0:512], in0=pO1[:, 0:512],
                                       scalar=ALPHA, in1=x1ch[:, 0:512],
                                       op0=ALU.mult, op1=ALU.add)
                return v.scalar_tensor_tensor(out=attn[:, 512:1024], in0=pO2[:, 0:512],
                                              scalar=ALPHA, in1=x1ch[:, 512:1024],
                                              op0=ALU.mult, op1=ALU.add)
            b.ve(mg, r=["pO12", "hi32"], w=["attn"])
            b.ve(lambda v: v.tensor_tensor(out=xa32[:], in0=attn[:], in1=xch[:],
                                           op=ALU.add),
                 r=["attn", "jsq"], w=["xa32"])
            b.ve(lambda v: v.tensor_tensor(out=attn[:], in0=xa32[:], in1=o2ch[:],
                                           op=ALU.add),
                 r=["xa32", "w32stg"], w=["attn"])

            if c == WCH0 and dbg:
                DUMP("attn", attn[:])

            def ln(inb_, outf, inn, outn, outb_=None, obn=None):
                b.sc(lambda s: s.activation(out=jsq[:], in_=inb_[:], func=AF.Square),
                     r=[inn], w=["jsq"])
                b.ve(lambda v: v.reduce_sum(col1[:], inb_[:], axis=mybir.AxisListType.X),
                     r=[inn], w=["c1"])
                b.ve(lambda v: v.reduce_sum(col2[:], jsq[:], axis=mybir.AxisListType.X),
                     r=["jsq"], w=["c2"])
                b.ve(lambda v: v.tensor_scalar(out=col3[:], in0=col1[:],
                                               scalar1=1.0 / DM, scalar2=None,
                                               op0=ALU.mult),
                     r=["c1"], w=["c3"])
                b.ve(lambda v: v.tensor_scalar(out=col4[:], in0=col2[:],
                                               scalar1=1.0 / DM, scalar2=None,
                                               op0=ALU.mult),
                     r=["c2"], w=["c4"])
                b.ve(lambda v: v.tensor_tensor(out=col5[:], in0=col3[:], in1=col3[:],
                                               op=ALU.mult),
                     r=["c3"], w=["c5"])
                b.ve(lambda v: v.tensor_tensor(out=col2[:], in0=col4[:], in1=col5[:],
                                               op=ALU.subtract),
                     r=["c4", "c5"], w=["c2"])
                b.sc(lambda s: s.activation(out=col4[:], in_=col2[:], func=AF.Sqrt,
                                            bias=epscol[:]),
                     r=["c2"], w=["c4"])
                b.ve(lambda v: v.reciprocal(out=col5[:], in_=col4[:]),
                     r=["c4"], w=["c5"])
                b.ve(lambda v: v.tensor_scalar(out=outf[:], in0=inb_[:],
                                               scalar1=col3[:], scalar2=col5[:],
                                               op0=ALU.subtract, op1=ALU.mult),
                     r=[inn, "c3", "c5"], w=[outn])
                if outb_ is not None:
                    b.ve(lambda v: v.tensor_copy(outb_[:], outf[:]),
                         r=[outn], w=[obn])

            xabf = rkg
            xafm = wktm
            xafm_lo = vgw
            h1fm = rkb
            h1lo = vbf
            h2fm = wkb
            ln(attn, xa32, "attn", "xa32", xabf, "rkg")
            b.ve(lambda v: v.tensor_copy(hi32stg[:], xabf[:]),
                 r=["rkg"], w=["hi32"])
            b.ve(lambda v: v.tensor_tensor(out=xalo[:], in0=xa32[:], in1=hi32stg[:],
                                           op=ALU.subtract),
                 r=["xa32", "hi32"], w=["xalo"])

            def ftr(t):
                last = None
                for mt in range(NMT):
                    last = t.transpose(pT[:, mt * 128:(mt + 1) * 128],
                                       xabf[:, mt * 128:(mt + 1) * 128], ident[:])
                return last
            b.te(ftr, r=["rkg"], w=["pT"])
            b.ve(lambda v: v.tensor_copy(xafm[:], pT[:, 0:DM]),
                 r=["pT"], w=["wktm"])

            def ftr2(t):
                last = None
                for mt in range(NMT):
                    last = t.transpose(pT[:, mt * 128:(mt + 1) * 128],
                                       xalo[:, mt * 128:(mt + 1) * 128], ident[:])
                return last
            b.te(ftr2, r=["xalo"], w=["pT"])
            b.ve(lambda v: v.tensor_copy(xafm_lo[:], pT[:, 0:DM]),
                 r=["pT"], w=["vgw"])

            def fmm1(t):
                last = None
                for m in range(NMT):
                    pD = (pO3, pO4)[m // 4]
                    sl = slice((m % 4) * 128, (m % 4) * 128 + 128)
                    for k in range(NMT):
                        t.matmul(pD[:, sl], W5(2, k)[:, m * 128:(m + 1) * 128],
                                 xafm[:, k * 128:(k + 1) * 128],
                                 start=(k == 0), stop=False)
                        last = t.matmul(pD[:, sl], W5(2, k)[:, m * 128:(m + 1) * 128],
                                        xafm_lo[:, k * 128:(k + 1) * 128],
                                        start=False, stop=(k == NMT - 1))
                return last
            b.te(fmm1, r=["wktm", "vgw", "wbig"], w=["pO34"])

            def fe1(v):
                v.tensor_copy(h1fm[:, 0:512], pO3[:, 0:512])
                return v.tensor_copy(h1fm[:, 512:1024], pO4[:, 0:512])
            b.ve(fe1, r=["pO34"], w=["rkb"])
            b.ve(lambda v: v.tensor_copy(hi32stg[:, 0:512], pO3[:, 0:512]),
                 r=["pO34"], w=["hi32"])
            b.ve(lambda v: v.tensor_copy(hi32stg[:, 512:1024], pO4[:, 0:512]),
                 r=["pO34", "hi32"], w=["hi32"])
            b.ve(lambda v: v.tensor_copy(w32stg[:], h1fm[:]),
                 r=["rkb"], w=["w32stg"])
            b.ve(lambda v: v.tensor_tensor(out=h1lo[:], in0=hi32stg[:], in1=w32stg[:],
                                           op=ALU.subtract),
                 r=["hi32", "w32stg"], w=["vbf"])

            def fmm2(t):
                last = None
                for m in range(NMT):
                    pD = (pO3, pO4)[m // 4]
                    sl = slice((m % 4) * 128, (m % 4) * 128 + 128)
                    for k in range(NMT):
                        t.matmul(pD[:, sl], W5(3, k)[:, m * 128:(m + 1) * 128],
                                 h1fm[:, k * 128:(k + 1) * 128],
                                 start=(k == 0), stop=False)
                        last = t.matmul(pD[:, sl], W5(3, k)[:, m * 128:(m + 1) * 128],
                                        h1lo[:, k * 128:(k + 1) * 128],
                                        start=False, stop=(k == NMT - 1))
                return last
            b.te(fmm2, r=["rkb", "vbf", "wbig"], w=["pO34"])

            def fe2(v):
                v.tensor_copy(h2fm[:, 0:512], pO3[:, 0:512])
                return v.tensor_copy(h2fm[:, 512:1024], pO4[:, 0:512])
            b.ve(fe2, r=["pO34"], w=["wkb"])
            b.ve(lambda v: v.tensor_copy(hi32stg[:, 0:512], pO3[:, 0:512]),
                 r=["pO34"], w=["hi32"])
            b.ve(lambda v: v.tensor_copy(hi32stg[:, 512:1024], pO4[:, 0:512]),
                 r=["pO34", "hi32"], w=["hi32"])
            b.ve(lambda v: v.tensor_copy(w32stg[:], h2fm[:]),
                 r=["wkb"], w=["w32stg"])
            b.ve(lambda v: v.tensor_tensor(out=h2lo[:], in0=hi32stg[:], in1=w32stg[:],
                                           op=ALU.subtract),
                 r=["hi32", "w32stg"], w=["h2lo"])

            def fmm3(t):
                last = None
                for ng in range(2):
                    pD = (pO1, pO2)[ng]
                    for k in range(NMT):
                        t.matmul(pD[:, 0:512], h2fm[:, k * 128:(k + 1) * 128],
                                 W5(4, k)[:, ng * 512:(ng + 1) * 512],
                                 start=(k == 0), stop=False)
                        t.matmul(pD[:, 0:512], h2lo[:, k * 128:(k + 1) * 128],
                                 W5(4, k)[:, ng * 512:(ng + 1) * 512],
                                 start=False, stop=False)
                    last = t.matmul(pD[:, 0:512], ones3[0:1, :],
                                    fbr2[0:1, ng * 512:(ng + 1) * 512],
                                    start=False, stop=True)
                return last
            b.te(fmm3, r=["wkb", "h2lo", "wbig"], w=["pO12"])

            def tnh(s):
                s.activation(out=attn[:, 0:512], in_=pO1[:, 0:512], func=AF.Tanh)
                return s.activation(out=attn[:, 512:1024], in_=pO2[:, 0:512],
                                    func=AF.Tanh)
            b.sc(tnh, r=["pO12"], w=["attn"])
            b.ve(lambda v: v.tensor_tensor(out=o2ch[:], in0=attn[:], in1=xa32[:],
                                           op=ALU.add),
                 r=["attn", "xa32"], w=["w32stg"])
            ln(o2ch, xch, "w32stg", "jsq")
            if c == WCH0 and dbg:
                DUMP("xa32", xa32[:])
                DUMP("ln2in", o2ch[:])
                DUMP("fin", xch[:])
            b.store(o[(c - WCH0) * MAC:(c - WCH0 + 1) * MAC, :], xch[:],
                    r=["jsq"], w=["d:o"])

        if dbg:
            b.store(dbg_aps["d_x1"][:, :], x1sp[:, :])
            b.store(dbg_aps["d_o2tm"][:, :], o2tm[:, :])
            b.store(dbg_aps["d_o2fm"][:, :], o2fm[:, :])

        b.finish()
        dump_log = dump_state["log"]
    return nc, {"dbg": dbg_names, "log": dump_log}


def prep_shared(ins):
    bf = ml_dtypes.bfloat16
    g = lambda k: np.ascontiguousarray(np.asarray(ins["L0_attn_" + k], np.float32))
    f = lambda k: np.ascontiguousarray(np.asarray(ins["L0_ffn_" + k], np.float32))
    d = {}
    d["in_wT"] = np.ascontiguousarray(g("in_proj_w").T).astype(bf)
    d["in_b_row"] = g("in_proj_b")[None, :].astype(bf)
    d["inb_c"] = np.ascontiguousarray(g("in_proj_b").reshape(NOC, 128).T)
    d["inb_r32"] = g("in_proj_b")[None, :].copy()
    d["outb32"] = g("out_proj_b")[None, :].copy()
    d["fb32"] = np.ascontiguousarray(np.asarray(ins["L0_ffn_bias"], np.float32))[None, :]
    d["ones32"] = np.ones((1, 128), np.float32)
    cw = g("conv_w")[:, 0, :]
    d["convw_c"] = np.ascontiguousarray(cw.reshape(NDT, 128, 4).transpose(1, 0, 2)
                                        .reshape(128, NDT * 4))
    d["convb_c"] = np.ascontiguousarray(g("conv_b").reshape(NDT, 128).T)
    xp = g("x_proj_w")
    pad = np.zeros((DI, 128), np.float32)
    pad[:, 0:DS] = xp[DR:DR + DS].T
    pad[:, 32:32 + DS] = xp[DR + DS:DR + 2 * DS].T
    pad[:, 64:128] = xp[0:DR].T
    d["xp_pad"] = pad
    d["dt_wTi"] = np.ascontiguousarray(g("dt_proj_w").T).astype(bf)
    d["dtb_c"] = np.ascontiguousarray(g("dt_proj_b").reshape(NDT, 128).T)
    d["alog_r"] = np.ascontiguousarray(g("A_log").reshape(NDT, 128, DS)
                                       .transpose(1, 0, 2).reshape(128, NDT * DS))
    d["d_c"] = np.ascontiguousarray(g("D_ssm").reshape(NDT, 128).T)
    d["nw1_c"] = np.ascontiguousarray(g("norm1_w").reshape(NMT, 128).T)
    d["nw2_c"] = np.ascontiguousarray(g("norm2_w").reshape(NMT, 128).T)
    d["out_wTi"] = np.ascontiguousarray(g("out_proj_w").T).astype(bf)
    d["out_b_row"] = g("out_proj_b")[None, :].astype(bf)
    d["wW"] = np.ascontiguousarray(g("proj_write_w").T)
    d["wR"] = np.ascontiguousarray(g("proj_read_w").T)
    d["fW1"] = f("weight")
    d["fW2"] = f("aux_weight1")
    d["fW3"] = f("aux_weight2")
    d["fb_row"] = f("bias")[None, :].astype(bf)
    d["decay"] = np.asarray(ins["L0_attn_decay"], np.float32).reshape(1, 1)
    p = np.arange(MAC)
    d["diffsT"] = np.ascontiguousarray(
        np.clip(p[None, :] - 1 - p[:, None], 0, None).astype(np.float32))
    d["causT"] = np.ascontiguousarray((p[None, :] > p[:, None]).astype(np.float32))
    d["p_row"] = p.astype(np.float32)[None, :]
    d["cp_col"] = (MAC - 1 - p).astype(np.float32)[:, None]
    d["ident_in"] = np.eye(128, dtype=np.float32).astype(bf)
    d["ones_row"] = np.ones((1, 128), np.float32).astype(bf)
    return d


def prep_core(ins, T, bidx, role):
    bf = ml_dtypes.bfloat16
    x = np.asarray(ins["L0_attn_x"], np.float32)[bidx]
    H = T // 2
    if role == 0:
        xbuf = np.concatenate([np.zeros((H, DM), np.float32), x[:H]], 0)
        m = np.concatenate([np.zeros(H, np.float32), np.ones(H, np.float32)])
    else:
        xbuf = x[:T]
        m = np.ones(T, np.float32)
    d = {"xin": np.ascontiguousarray(xbuf),
         "mrow_16": m[None, :].astype(np.float16),
         "mcols": np.ascontiguousarray(m.reshape(T // 128, 128).T)}
    return d


_CACHE = {}


def kernel(**inputs):
    T = 4096
    if "nc" not in _CACHE:
        nc, _ = build(T)
        _CACHE["nc"] = nc
    nc = _CACHE["nc"]
    shared = prep_shared(inputs)
    in_maps = []
    for c in range(8):
        m = dict(shared)
        m.update(prep_core(inputs, T, c // 2, c % 2))
        in_maps.append(m)
    res = run_bass_kernel_spmd(nc, in_maps, core_ids=list(range(8)))
    out = np.empty((4, T, DM), np.float32)
    for bidx in range(4):
        out[bidx, :T // 2] = np.asarray(res.results[2 * bidx]["o"])
        out[bidx, T // 2:] = np.asarray(res.results[2 * bidx + 1]["o"])
    return out


# version 24
# speedup vs baseline: 1.1557x; 1.1557x over previous
import numpy as np
import ml_dtypes
from contextlib import ExitStack

PUMP = True
OMM_TAIL = True
PUMP_CLASSES = {'n1', 'n2', 'w', 'm', 'e1', 'e2'}

import concourse.bass as bass
import concourse.mybir as mybir
from concourse.bass_utils import run_bass_kernel_spmd

F32 = mybir.dt.float32
F16 = mybir.dt.float16
BF16 = mybir.dt.bfloat16
AF = mybir.ActivationFunctionType
ALU = mybir.AluOpType

DM, DI, DS, DR = 1024, 2048, 16, 64
NDT = DI // 128
NMT = DM // 128
NOC = 2 * DI // 128
ALPHA = 0.1
EPS = 1e-5
MAC = 128


class Ctr:
    def __init__(self, es, nc, name):
        self.h = es.enter_context(nc.semaphore(name))
        self.n = 0

    def inc(self, inst, k=1):
        inst.then_inc(self.h, k)
        self.n += k
        return inst


class Bld:

    def __init__(self, nc, es):
        self.nc = nc
        self.block = es.enter_context(nc.Block())
        self.cv = Ctr(es, nc, "cv")
        self.cs = Ctr(es, nc, "cs")
        self.ct = Ctr(es, nc, "ct")
        self.cp = Ctr(es, nc, "cp")
        self.ql = Ctr(es, nc, "ql")
        self.qs = Ctr(es, nc, "qs")
        self.qg = Ctr(es, nc, "qg")
        self._pend = []
        self.reg = {}
        self.bar = None
        self.covered = {}

    def _deps_for(self, reads, writes):
        deps = []
        for bn in reads:
            e = self.reg.get(bn)
            if e and e["w"]:
                deps.append(e["w"])
        for bn in writes:
            e = self.reg.get(bn)
            if e:
                if e["w"]:
                    deps.append(e["w"])
                deps.extend(e["r"])
        if self.bar is not None:
            deps.append(self.bar)
        return deps

    def _post(self, tok, reads, writes):
        for bn in reads:
            self.reg.setdefault(bn, {"w": None, "r": []})["r"].append(tok)
        for bn in writes:
            self.reg[bn] = {"w": tok, "r": []}

    @staticmethod
    def _collapse(deps):
        m = {}
        for ctr, n in deps:
            if ctr is not None:
                m[id(ctr)] = (ctr, max(n, m.get(id(ctr), (ctr, 0))[1]))
        return list(m.values())

    def _emit_waits(self, eng, deps, stream=None):
        cov = self.covered.setdefault(stream, {}) if stream else None
        for ctr, n in self._collapse(deps):
            if cov is not None:
                if cov.get(id(ctr), -1) >= n:
                    continue
                cov[id(ctr)] = n
            eng.wait_ge(ctr.h, n)

    def _stage(self, emitfn, engfn, ctr, reads, writes, stream=None):
        if reads is None and writes is None:
            def body(e):
                cov = self.covered.setdefault(stream, {}) if stream else {}
                for c2 in (self.cv, self.cs, self.ct, self.cp,
                           self.ql, self.qs, self.qg):
                    e.wait_ge(c2.h, c2.n)
                    cov[id(c2)] = c2.n
                for c2, n2 in self._pend:
                    e.wait_ge(c2.h, n2)
                    cov[id(c2)] = max(cov.get(id(c2), -1), n2)
                self._pend = []
                ctr.inc(engfn(e))
            emitfn(body)
            self.bar = (ctr, ctr.n)
            for bn in list(self.reg):
                self.reg[bn] = {"w": self.bar, "r": []}
            return self.bar
        reads = reads or []
        writes = writes or []
        deps = self._deps_for(reads, writes) + self._pend
        self._pend = []

        def body(e):
            self._emit_waits(e, deps, stream)
            ctr.inc(engfn(e))
        emitfn(body)
        tok = (ctr, ctr.n)
        self._post(tok, reads, writes)
        return tok

    def ve(self, fn, r=None, w=None):
        return self._stage(self.block.vector, fn, self.cv, r, w, "v")

    def sc(self, fn, r=None, w=None):
        return self._stage(self.block.scalar, fn, self.cs, r, w, "s")

    def te(self, fn, r=None, w=None):
        return self._stage(self.block.tensor, fn, self.ct, r, w, "t")

    def gp(self, fn, r=None, w=None):
        return self._stage(self.block.gpsimd, fn, self.cp, r, w, "p")

    def load(self, out, in_, q=None, slow=False, after=(), r=None, w=None):
        ctr = q if q is not None else self.ql
        if r is None and w is None:
            cv_n, cs_n, ct_n = self.cv.n, self.cs.n, self.ct.n
            extra = tuple(after)

            def body(e):
                e.wait_ge(self.cv.h, cv_n)
                e.wait_ge(self.cs.h, cs_n)
                e.wait_ge(self.ct.h, ct_n)
                for c2, n2 in extra:
                    e.wait_ge(c2.h, n2)
                ctr.inc(e.dma_start(out=out, in_=in_,
                                    allow_slow_non_contiguous=slow), 16)
            if ctr is self.ql:
                self.block.sync(body)
            elif ctr is self.qg:
                self.block.gpsimd(body)
            else:
                self.block.scalar(body)
            return (ctr, ctr.n)
        r = r or []
        w = w or []
        deps = self._deps_for(r, w) + list(after)
        stream = "sync" if ctr is self.ql else ("g" if ctr is self.qg else "s")

        def body(e):
            self._emit_waits(e, deps, stream)
            ctr.inc(e.dma_start(out=out, in_=in_,
                                allow_slow_non_contiguous=slow), 16)
        if ctr is self.ql:
            self.block.sync(body)
        elif ctr is self.qg:
            self.block.gpsimd(body)
        else:
            self.block.scalar(body)
        tok = (ctr, ctr.n)
        self._post(tok, r or [], w or [])
        return tok

    def store(self, out, in_, slow=False, r=None, w=None):
        return self.load(out, in_, q=self.qs, slow=slow, r=r, w=w)

    def want(self, tok):
        self._pend.append(tok)

    def finish(self):
        qs_n, ql_n, qg_n = self.qs.n, self.ql.n, self.qg.n
        cv_n, cs_n, ct_n, cp_n = self.cv.n, self.cs.n, self.ct.n, self.cp.n

        def body(e):
            e.wait_ge(self.cv.h, cv_n)
            e.wait_ge(self.cs.h, cs_n)
            e.wait_ge(self.ct.h, ct_n)
            e.wait_ge(self.cp.h, cp_n)
            e.wait_ge(self.qs.h, qs_n)
            e.wait_ge(self.ql.h, ql_n)
            e.wait_ge(self.qg.h, qg_n)
        self.block.sync(body)


def build(T, TT=256, dbg=False):
    WIN0 = T // 2
    NT = T // TT
    TCH = TT // 128
    BL = TT + 2
    CBL = TT + 4
    NCH = T // MAC
    WCH0 = WIN0 // MAC

    nc = bass.Bass("TRN2", target_bir_lowering=False)

    def din(name, shape, dt):
        return nc.dram_tensor(name, list(shape), dt, kind="ExternalInput").ap()

    xin = din("xin", [T, DM], F32)
    mrow_16 = din("mrow_16", [1, T], F16)
    mcols = din("mcols", [128, T // 128], F32)
    in_wTr = din("in_wTr", [NOC * 128, NMT * 128], BF16)
    in_b_row = din("in_b_row", [1, 2 * DI], BF16)
    inb_c = din("inb_c", [128, NOC], F32)
    inb_r32 = din("inb_r32", [1, NOC * 128], F32)
    outb32 = din("outb32", [1, DM], F32)
    fb32 = din("fb32", [1, DM], F32)
    ones32 = din("ones32", [1, 128], F32)
    convw_c = din("convw_c", [128, NDT * 4], F32)
    convb_c = din("convb_c", [128, NDT], F32)
    xp_pad = din("xp_pad", [DI, 128], F32)
    dt_wTi = din("dt_wTi", [DR, DI], BF16)
    dtb_c = din("dtb_c", [128, NDT], F32)
    alog_r = din("alog_r", [128, NDT * DS], F32)
    d_c = din("d_c", [128, NDT], F32)
    nw1_c = din("nw1_c", [128, NMT], F32)
    nw2_c = din("nw2_c", [128, NMT], F32)
    out_wTi = din("out_wTi", [DI, DM], BF16)
    out_b_row = din("out_b_row", [1, DM], BF16)
    wW = din("wW", [DM, DM], F32)
    wR = din("wR", [DM, DM], F32)
    fW1 = din("fW1", [DM, DM], F32)
    fW2 = din("fW2", [DM, DM], F32)
    fW3 = din("fW3", [DM, DM], F32)
    fb_row = din("fb_row", [1, DM], BF16)
    decay = din("decay", [1, 1], F32)
    diffsT = din("diffsT", [MAC, MAC], F32)
    causT = din("causT", [MAC, MAC], F32)
    p_row = din("p_row", [1, MAC], F32)
    cp_col = din("cp_col", [MAC, 1], F32)
    ident_in = din("ident_in", [128, 128], BF16)
    ones_row = din("ones_row", [1, 128], BF16)

    o = nc.dram_tensor("o", [T - WIN0, DM], F32, kind="ExternalOutput").ap()

    x1sp = nc.dram_tensor("x1sp", [T, DM], F32, kind="Internal").ap()
    o2tm = nc.dram_tensor("o2tm", [T, DM], F32, kind="Internal").ap()
    o2fm = nc.dram_tensor("o2fm", [DM, T + 1], BF16, kind="Internal").ap()
    bsp = nc.dram_tensor("bsp", [1, DS, T], F16, kind="Internal").ap()
    csp = nc.dram_tensor("csp", [1, DS, T], F16, kind="Internal").ap()
    wlo = nc.dram_tensor("wlo", [5, DM, DM], BF16, kind="Internal").ap()
    dbg_names = []
    dbg_aps = {}
    if dbg:
        for nm, shape, dt in [("d_x1", [T, DM], F32), ("d_o2tm", [T, DM], F32),
                              ("d_o2fm", [DM, T + 1], BF16)]:
            dbg_aps[nm] = nc.dram_tensor(nm, shape, dt, kind="ExternalOutput").ap()
            dbg_names.append(nm)
        dbg_aps["dump"] = nc.dram_tensor("dump", [64, 128, 4224], F16,
                                         kind="ExternalOutput").ap()

    with ExitStack() as es:
        E = es.enter_context
        b = Bld(nc, es)
        dump_state = {"i": 0, "log": [], "buf": None}

        def DUMP(tag, ap):
            if not dbg or dump_state["buf"] is None:
                return
            dmp32 = dump_state["buf"]
            i = dump_state["i"]
            if i >= 64:
                return
            dump_state["i"] += 1
            dump_state["log"].append(tag)
            n = ap.shape[-1] if len(ap.shape) == 2 else ap.free_size()
            n = min(n, 2176)
            p = ap.shape[0]
            b.want(dump_state.get("tok") or (b.qs, 0))
            b.ve(lambda v: v.memset(dmp32[:], 0.0))
            b.want(dump_state.get("tok") or (b.qs, 0))
            b.ve(lambda v: v.tensor_copy(dmp32[0:p, 0:n], ap[:, 0:n] if len(ap.shape) == 2 else ap))
            dump_state["tok"] = b.store(dbg_aps["dump"][i, :, 0:2176], dmp32[:])

        mcol = E(nc.sbuf_tensor("mcol", [128, T // 128], F32))
        convw = E(nc.sbuf_tensor("convw", [128, NDT * 4], F32))
        convb = E(nc.sbuf_tensor("convb", [128, NDT], F32))

        dtw = E(nc.sbuf_tensor("dtw", [DR, DI], BF16))
        dtb = E(nc.sbuf_tensor("dtb", [128, NDT], F32))
        inbr = E(nc.sbuf_tensor("inbr", [1, NOC * 128], F32))
        ones2r = E(nc.sbuf_tensor("ones2r", [1, TT], F32))
        asb = E(nc.sbuf_tensor("asb", [128, NDT * DS], F32))
        dcol = E(nc.sbuf_tensor("dcol", [128, NDT], F32))
        nw1 = E(nc.sbuf_tensor("nw1", [128, NMT], F32))
        nw2 = E(nc.sbuf_tensor("nw2", [128, NMT], F32))
        inbc = E(nc.sbuf_tensor("inbc", [128, NOC], F32))
        outb2 = E(nc.sbuf_tensor("outb2", [1, DM], F32))
        fbr2 = E(nc.sbuf_tensor("fbr2", [1, DM], F32))
        ones3 = E(nc.sbuf_tensor("ones3", [1, 128], F32))
        ident = E(nc.sbuf_tensor("ident", [128, 128], BF16))
        ones1 = E(nc.sbuf_tensor("ones1", [1, 128], BF16))
        dk = E(nc.sbuf_tensor("dk", [1, 1], F32))
        gam = E(nc.sbuf_tensor("gam", [1, 1], F32))
        logg1 = E(nc.sbuf_tensor("logg1", [1, 1], F32))
        loggb = E(nc.sbuf_tensor("loggb", [1, 1], BF16))
        loggc = E(nc.sbuf_tensor("loggc", [128, 1], F32))
        gin_row = E(nc.sbuf_tensor("gin_row", [1, MAC], F32))
        gin_rowb = E(nc.sbuf_tensor("gin_rowb", [1, MAC], BF16))
        gin_bc = E(nc.sbuf_tensor("gin_bc", [128, MAC], BF16))
        gin_lo = E(nc.sbuf_tensor("gin_lo", [128, MAC], BF16))
        ginr32 = E(nc.sbuf_tensor("ginr32", [1, MAC], F32))
        gin_row2 = E(nc.sbuf_tensor("gin_row2", [1, MAC], F32))
        gw_col = E(nc.sbuf_tensor("gw_col", [MAC, 1], F32))
        gc_col = E(nc.sbuf_tensor("gc_col", [128, 1], F32))
        c128 = E(nc.sbuf_tensor("c128", [128, 1], F32))
        cpc = E(nc.sbuf_tensor("cpc", [MAC, 1], F32))
        dTt = E(nc.sbuf_tensor("dTt", [MAC, MAC], F32))
        cTt = E(nc.sbuf_tensor("cTt", [MAC, MAC], F32))
        mTt = E(nc.sbuf_tensor("mTt", [MAC, MAC], F32))
        gwm = E(nc.sbuf_tensor("gwm", [MAC, 1], F32))
        jsq = E(nc.sbuf_tensor("jsq", [128, DM], F32))
        col1 = E(nc.sbuf_tensor("col1", [128, 1], F32))
        col2 = E(nc.sbuf_tensor("col2", [128, 1], F32))
        col3 = E(nc.sbuf_tensor("col3", [128, 1], F32))
        col4 = E(nc.sbuf_tensor("col4", [128, 1], F32))
        col5 = E(nc.sbuf_tensor("col5", [128, 1], F32))
        zcol = E(nc.sbuf_tensor("zcol", [128, 1], BF16))
        epscol = E(nc.sbuf_tensor("epscol", [128, 1], F32))
        onecol = E(nc.sbuf_tensor("onecol", [128, 1], F32))
        zerocol = E(nc.sbuf_tensor("zerocol", [128, 1], F32))
        carry = E(nc.sbuf_tensor("carry", [128, NDT * DS], F16))

        if dbg and False:
            dump_state["buf"] = E(nc.sbuf_tensor("dmp32", [128, 2176], F16))
        es2 = ExitStack()
        E2 = es2.enter_context
        xpw = E2(nc.sbuf_tensor("xpw", [128, NDT * 128], BF16))
        xpl = E2(nc.sbuf_tensor("xpl", [128, NDT * 128], BF16))
        mtile = E2(nc.sbuf_tensor("mtile", [128, 2 * TT], F16))
        xtm = E2(nc.sbuf_tensor("xtm", [128, DM], F32))
        xrfm = E2(nc.sbuf_tensor("xrfm", [128, NMT * TT], BF16))
        wstage = E2(nc.sbuf_tensor("wstage", [128, NMT * 128], BF16))
        wstage2 = E2(nc.sbuf_tensor("wstage2", [128, NMT * 128], BF16))
        ostage = E2(nc.sbuf_tensor("ostage", [128, DM], BF16))
        ostage2 = E2(nc.sbuf_tensor("ostage2", [128, DM], BF16))
        xzc = E2(nc.sbuf_tensor("xzc", [128, NDT * CBL], F16))
        silu_r = E2(nc.sbuf_tensor("silu_r", [128, NDT * TT], F16))
        xm16 = E2(nc.sbuf_tensor("xm16", [128, NDT * TT], F16))
        xmbf = E2(nc.sbuf_tensor("xmbf", [128, NDT * TT], BF16))
        xmlo = E2(nc.sbuf_tensor("xmlo", [128, NDT * TT], BF16))
        del32 = E2(nc.sbuf_tensor("del32", [128, NDT * TT], F32))
        dtmp = E2(nc.sbuf_tensor("dtmp", [128, TT], F16))
        xb16 = E2(nc.sbuf_tensor("xb16", [DS, TT], F16))
        xc16 = E2(nc.sbuf_tensor("xc16", [DS, TT], F16))
        xdt = E2(nc.sbuf_tensor("xdt", [DR, TT], BF16))
        bbc = E2(nc.sbuf_tensor("bbc", [128, DS * TT], F16))
        cbc = E2(nc.sbuf_tensor("cbc", [128, DS * TT], F16))
        dab = E2(nc.sbuf_tensor("dab", [128, DS * BL], F16))
        dab2 = dab if dbg else E2(nc.sbuf_tensor("dab2", [128, DS * BL], F16))
        dbb = E2(nc.sbuf_tensor("dbb", [128, DS * BL], F16))
        hbb = E2(nc.sbuf_tensor("hbb", [128, DS * BL], F16))
        hbb2 = E2(nc.sbuf_tensor("hbb2", [128, DS * BL], F16))
        hcb = E2(nc.sbuf_tensor("hcb", [128, DS * TT], F16))
        ys16 = E2(nc.sbuf_tensor("ys16", [128, TT], F16))
        ysum = E2(nc.sbuf_tensor("ysum", [128, TT], F16))
        yt1 = E2(nc.sbuf_tensor("yt1", [128, TT], F16))
        yfin = E2(nc.sbuf_tensor("yfin", [128, NDT * TT], BF16))
        x1tm = E2(nc.sbuf_tensor("x1tm", [128, DM], F32))
        xres = E2(nc.sbuf_tensor("xres", [128, DM], F32))
        o2bfb = E2(nc.sbuf_tensor("o2bfb", [128, DM], BF16))
        o2fmb = o2bfb

        pA = E(nc.psum_tensor("pA", [128, 512], F32))
        pY = E(nc.psum_tensor("pY", [128, 512], F32))
        pO1 = E(nc.psum_tensor("pO1", [128, 512], F32))
        pO2 = E(nc.psum_tensor("pO2", [128, 512], F32))
        pO3 = E(nc.psum_tensor("pO3", [128, 512], F32))
        pO4 = E(nc.psum_tensor("pO4", [128, 512], F32))
        pT = E(nc.psum_tensor("pT", [128, 1024], BF16))

        def W5(w, k):
            return wbig[:, (w * NMT + k) * DM:(w * NMT + k + 1) * DM]

        for tok in [
            b.load(mcol[:], mcols[:, :]),
            b.load(convw[:], convw_c[:, :]),
            b.load(convb[:], convb_c[:, :]),

            b.load(dtw[:], dt_wTi[:, :]),
            b.load(dtb[:], dtb_c[:, :]),
            b.load(asb[:], alog_r[:, :]),
            b.load(dcol[:], d_c[:, :]),
            b.load(nw1[:], nw1_c[:, :]),
            b.load(nw2[:], nw2_c[:, :]),
            b.load(inbc[:], inb_c[:, :]),
            b.load(inbr[:], inb_r32[:, :]),
            b.load(outb2[:], outb32[:, :]),
            b.load(fbr2[:], fb32[:, :]),
            b.load(ones3[:], ones32[:, :]),
            b.load(ident[:], ident_in[:, :]),
            b.load(ones1[:], ones_row[:, :]),
            b.load(dk[:], decay[:, :]),
            b.load(dTt[:], diffsT[:, :]),
            b.load(cTt[:], causT[:, :]),
            b.load(gin_row[:], p_row[:, :]),
            b.load(cpc[:], cp_col[:, :]),
        ]:
            b.want(tok)

        b.sc(lambda s: s.activation(out=asb[:], in_=asb[:], func=AF.Exp))
        b.ve(lambda v: v.tensor_scalar(out=asb[:], in0=asb[:], scalar1=-1.0,
                                       scalar2=None, op0=ALU.mult))
        b.sc(lambda s: s.activation(out=gam[:], in_=dk[:], func=AF.Sigmoid))
        b.sc(lambda s: s.activation(out=logg1[:], in_=gam[:], func=AF.Ln))
        b.ve(lambda v: v.tensor_copy(loggb[:], logg1[:]))
        b.te(lambda t: t.matmul(pA[:, 0:1], ones1[0:1, :], loggb[0:1, :],
                                start=True, stop=True))
        b.ve(lambda v: v.tensor_copy(loggc[:], pA[:, 0:1]))
        b.sc(lambda s: s.activation(out=gin_row[:], in_=gin_row[:], func=AF.Exp,
                                    scale=logg1[:]))
        b.ve(lambda v: v.tensor_copy(gin_rowb[:], gin_row[:]))
        b.te(lambda t: t.matmul(pA[:, 0:MAC], ones1[0:1, :], gin_rowb[0:1, :],
                                start=True, stop=True))
        b.ve(lambda v: v.tensor_copy(gin_bc[:], pA[:, 0:MAC]))
        b.ve(lambda v: v.tensor_copy(ginr32[:], gin_rowb[:]))
        b.ve(lambda v: v.tensor_tensor(out=gin_row2[:], in0=gin_row[:], in1=ginr32[:],
                                       op=ALU.subtract))
        b.ve(lambda v: v.tensor_copy(gin_rowb[:], gin_row2[:]))
        b.te(lambda t: t.matmul(pA[:, 0:MAC], ones1[0:1, :], gin_rowb[0:1, :],
                                start=True, stop=True))
        b.ve(lambda v: v.tensor_copy(gin_lo[:], pA[:, 0:MAC]))
        b.sc(lambda s: s.activation(out=gw_col[:], in_=cpc[:], func=AF.Exp,
                                    scale=loggc[0:MAC, :]))
        b.ve(lambda v: v.memset(c128[:], float(MAC)))
        b.sc(lambda s: s.activation(out=gc_col[:], in_=c128[:], func=AF.Exp,
                                    scale=loggc[:]))
        b.sc(lambda s: s.activation(out=dTt[:], in_=dTt[:], func=AF.Exp,
                                    scale=loggc[0:MAC, :]))
        b.ve(lambda v: v.tensor_tensor(out=mTt[:], in0=dTt[:], in1=cTt[:], op=ALU.mult))

        for k in range(NDT):
            tok = b.load(xtm[:, 0:128], xp_pad[k * 128:(k + 1) * 128, :])
            b.want(tok)
            b.ve(lambda v, k=k: v.tensor_copy(xpw[:, k * 128:(k + 1) * 128],
                                              xtm[:, 0:128]))
            b.ve(lambda v, k=k: v.tensor_copy(jsq[:, 0:128],
                                              xpw[:, k * 128:(k + 1) * 128]))
            b.ve(lambda v, k=k: v.tensor_tensor(out=xpl[:, k * 128:(k + 1) * 128],
                                                in0=xtm[:, 0:128], in1=jsq[:, 0:128],
                                                op=ALU.subtract))

        def init_cols(v):
            for buf in ([dab, dab2] if dab2 is not dab else [dab]):
                v.memset(buf[:].rearrange("p (s c) -> p s c", c=BL)[:, :, 0:1],
                         0.0)
                v.memset(buf[:].rearrange("p (s c) -> p s c", c=BL)[:, :, 1:2],
                         1.0)
            v.memset(dbb[:].rearrange("p (s c) -> p s c", c=BL)[:, :, 1:2], 0.0)
            v.memset(zcol[:], 0.0)
            v.memset(epscol[:], EPS)
            v.memset(onecol[:], 1.0)
            v.memset(ones2r[:], 1.0)
            return v.memset(zerocol[:], 0.0)
        b.ve(init_cols)
        for mt in range(NMT):
            b.store(o2fm[mt * 128:(mt + 1) * 128, 0:1], zcol[:], slow=True)

        def dn(d):
            return "del32a" if d < NDT // 2 else "del32b"

        def emit_call(call):
            src_t = xin if call == 1 else x1sp
            nw = nw1 if call == 1 else nw2
            PB = (pA, pY, pO1, pO2, pO3, pO4)
            PBN = ("pb0", "pb1", "pb2", "pb3", "pb4", "pb5")
            DL = ["dl%d" % d for d in range(NDT)]
            SL = ["sl%d" % d for d in range(NDT)]

            def reset(v):
                v.memset(xzc[:].rearrange("p (d c) -> p d c", c=CBL)[:, :, 0:4],
                         0.0)
                return v.memset(carry[:], 0.0)
            b.ve(reset)

            def produce_stages(it):
                t0 = it * TT
                mt = mtile[:, (it % 2) * TT:(it % 2) * TT + TT]
                mtn = "mt%d" % (it % 2)
                st = []

                def G(g, cls):
                    return g if cls in PUMP_CLASSES else NDT
                st.append((G(0, 'n1'), lambda: b.load(
                    mt, mrow_16[0:1, t0:t0 + TT].partition_broadcast(128),
                    q=b.qg, w=[mtn])))
                xsrc = [] if call == 1 else ["d:x1sp"]
                for tc in range(TCH):
                    st.append((G(0, 'n1'), lambda tc=tc: b.load(
                        xtm[:], src_t[t0 + tc * 128:t0 + (tc + 1) * 128, :],
                        q=b.qs, r=xsrc, w=["xtm"])))
                    st.append((G(0, 'n1'), lambda: b.sc(
                        lambda s: s.activation(out=jsq[:], in_=xtm[:],
                                               func=AF.Square),
                        r=["xtm"], w=["jsq"])))
                    st.append((G(0, 'n1'), lambda: b.ve(
                        lambda v: v.reduce_sum(col1[:], jsq[:],
                                               axis=mybir.AxisListType.X),
                        r=["jsq"], w=["c1"])))
                    st.append((G(0, 'n1'), lambda: b.sc(
                        lambda s: s.activation(out=col2[:], in_=col1[:],
                                               func=AF.Sqrt, scale=1.0 / DM,
                                               bias=epscol[:]),
                        r=["c1"], w=["c2"])))
                    st.append((G(0, 'n1'), lambda: b.ve(
                        lambda v: v.reciprocal(out=col3[:], in_=col2[:]),
                        r=["c2"], w=["c3"])))
                    st.append((G(0, 'n1'), lambda: b.sc(
                        lambda sc: sc.activation(out=o2bfb[:], in_=xtm[:],
                                                 func=AF.Copy, scale=col3[:]),
                        r=["xtm", "c3"], w=["o2bfb"])))

                    def transp2(t):
                        last = None
                        for k in range(NMT):
                            last = t.transpose(pT[:, k * 128:(k + 1) * 128],
                                               o2bfb[:, k * 128:(k + 1) * 128],
                                               ident[:])
                        return last
                    st.append((G(0, 'n2'), lambda transp2=transp2: b.te(
                        transp2, r=["o2bfb"], w=["pT"])))

                    def evtr(v, tc=tc):
                        last = None
                        for k in range(NMT):
                            last = v.tensor_scalar(
                                out=xrfm[:, k * TT + tc * 128:k * TT + (tc + 1) * 128],
                                in0=pT[:, k * 128:(k + 1) * 128],
                                scalar1=nw[:, k:k + 1], scalar2=None,
                                op0=ALU.mult)
                        return last
                    st.append((G(0, 'n2'), lambda evtr=evtr: b.ve(
                        evtr, r=["pT"], w=["xrfm"])))

                pos = 0
                first = {"v": True}

                def wload(oc, pos):
                    stage = wstage if pos % 2 == 0 else wstage2
                    wsn = "ws0" if pos % 2 == 0 else "ws1"
                    return (G(0, 'w'), lambda: b.load(
                        stage[:], in_wTr[oc * 128:(oc + 1) * 128, :], w=[wsn]))

                def mm_st(oc, pos, gate, bias=False):
                    stage = wstage if pos % 2 == 0 else wstage2
                    wsn = "ws0" if pos % 2 == 0 else "ws1"
                    pacc = PB[pos % 2]
                    pn = PBN[pos % 2]

                    def mm(t, oc=oc, stage=stage, pacc=pacc, bias=bias):
                        if bias:
                            t.matmul(pacc[:, 0:TT],
                                     inbr[0:1, oc * 128:(oc + 1) * 128],
                                     ones2r[0:1, 0:TT], start=True, stop=False)
                        last = None
                        for k in range(NMT):
                            last = t.matmul(pacc[:, 0:TT],
                                            stage[:, k * 128:(k + 1) * 128],
                                            xrfm[:, k * TT:(k + 1) * TT],
                                            start=(not bias and k == 0),
                                            stop=(k == NMT - 1))
                        return last
                    return (G(gate, 'm'), lambda mm=mm: b.te(mm, r=[wsn, "xrfm"],
                                                     w=[pn]))

                st.append(wload(0, 0))
                for j in range(NDT):
                    oc1, oc2 = j, j + NDT
                    p1, p2 = 2 * j, 2 * j + 1
                    st.append(mm_st(oc1, p1, max(0, j - 1)))
                    st.append(wload(oc2, p2))

                    def ev1(v, d=j, oc=oc1, mt=mt, pacc=PB[p1 % 2]):
                        return v.scalar_tensor_tensor(
                            out=xzc[:, d * CBL + 4:d * CBL + 4 + TT],
                            in0=pacc[:, 0:TT], scalar=inbc[:, oc:oc + 1],
                            in1=mt, op0=ALU.add, op1=ALU.mult)
                    st.append((G(max(0, j - 1), 'e1'), lambda ev1=ev1, pn=PBN[p1 % 2]:
                               b.ve(ev1, r=[pn, mtn], w=["xzc"])))
                    st.append(mm_st(oc2, p2, max(0, j - 1), bias=True))
                    if j + 1 < NDT:
                        st.append(wload(oc1 + 1, p2 + 1))

                    def ev2(s, d=j, pacc=PB[p2 % 2]):
                        return s.activation(out=silu_r[:, d * TT:(d + 1) * TT],
                                            in_=pacc[:, 0:TT], func=AF.Copy)
                    st.append((G(j + 1, 'e2'), lambda ev2=ev2, pn=PBN[p2 % 2], d=j:
                               b.sc(ev2, r=[pn], w=[SL[d]])))


                def hist(v):
                    return v.tensor_copy(
                        xzc[:].rearrange("p (d c) -> p d c", c=CBL)[:, :, 1:4],
                        xzc[:].rearrange("p (d c) -> p d c", c=CBL)
                        [:, :, 1 + TT:4 + TT])
                st.append((NDT, lambda: b.ve(hist, r=["xzc"], w=["xzc"])))
                return st

            def emit_mid(it):
                t0 = it * TT
                mt = mtile[:, (it % 2) * TT:(it % 2) * TT + TT]
                mtn = "mt%d" % (it % 2)
                for d in range(NDT):
                    base = d * CBL
                    b.ve(lambda v, base=base, d=d: v.tensor_scalar(
                        out=del32[:, d * TT:(d + 1) * TT],
                        in0=xzc[:, base + 1:base + 1 + TT],
                        scalar1=convw[:, d * 4:d * 4 + 1],
                        scalar2=convb[:, d:d + 1], op0=ALU.mult, op1=ALU.add),
                         r=["xzc"], w=[DL[d]])
                    for k in range(1, 4):
                        b.ve(lambda v, base=base, d=d, k=k: v.scalar_tensor_tensor(
                            out=del32[:, d * TT:(d + 1) * TT],
                            in0=xzc[:, base + 1 + k:base + 1 + k + TT],
                            scalar=convw[:, d * 4 + k:d * 4 + k + 1],
                            in1=del32[:, d * TT:(d + 1) * TT],
                            op0=ALU.mult, op1=ALU.add),
                             r=["xzc", DL[d]], w=[DL[d]])
                b.sc(lambda s: s.activation(out=silu_r[:], in_=silu_r[:],
                                            func=AF.Silu, scale=onecol[:],
                                            bias=zerocol[:]),
                     r=SL, w=SL)
                b.sc(lambda s: s.activation(out=xm16[:], in_=del32[:],
                                            func=AF.Silu, scale=onecol[:],
                                            bias=zerocol[:]),
                     r=DL, w=["xm16"])

                def mk1(v, mt=mt):
                    return v.tensor_tensor(
                        out=hcb[:].rearrange("p (d t) -> p d t", t=TT),
                        in0=mt[:, None, :].broadcast_to([128, NDT, TT]),
                        in1=xm16[:].rearrange("p (d t) -> p d t", t=TT),
                        op=ALU.mult)
                b.ve(mk1, r=["xm16", mtn], w=["hcb"])
                b.ve(lambda v: v.tensor_copy(xmbf[:], hcb[:]),
                     r=["hcb"], w=["xmbf"])
                b.ve(lambda v: v.tensor_tensor(out=xmlo[:], in0=hcb[:],
                                               in1=xmbf[:], op=ALU.subtract),
                     r=["hcb", "xmbf"], w=["xmlo"])

                def xproj(t):
                    last = None
                    for k in range(NDT):
                        t.matmul(pA[:, 0:TT], xpw[:, k * 128:(k + 1) * 128],
                                 xmbf[:, k * TT:(k + 1) * TT],
                                 start=(k == 0), stop=False)
                        t.matmul(pA[:, 0:TT], xpw[:, k * 128:(k + 1) * 128],
                                 xmlo[:, k * TT:(k + 1) * TT],
                                 start=False, stop=False)
                        last = t.matmul(pA[:, 0:TT], xpl[:, k * 128:(k + 1) * 128],
                                        xmbf[:, k * TT:(k + 1) * TT],
                                        start=False, stop=(k == NDT - 1))
                    return last
                b.te(xproj, r=["xmbf", "xmlo"], w=["pb0"])

                def bcd_ev(sc):
                    sc.activation(out=xb16[:], in_=pA[0:DS, 0:TT], func=AF.Copy)
                    sc.activation(out=xc16[:], in_=pA[32:32 + DS, 0:TT],
                                  func=AF.Copy)
                    return sc.activation(out=xdt[:], in_=pA[64:128, 0:TT],
                                         func=AF.Copy)
                b.sc(bcd_ev, r=["pb0"], w=["xb16", "xc16", "xdt"])
                b.load(bsp[0, :, t0:t0 + TT], xb16[:], q=b.qg,
                       r=["xb16"], w=["d:bsp"])
                b.load(csp[0, :, t0:t0 + TT], xc16[:], q=b.qg,
                       r=["xc16"], w=["d:csp"])
                b.load(bbc[:].rearrange("p (s t) -> p s t", t=TT),
                       bsp[0:1, :, t0:t0 + TT].partition_broadcast(128), q=b.qg,
                       r=["d:bsp"], w=["bbc"])
                b.load(cbc[:].rearrange("p (s t) -> p s t", t=TT),
                       csp[0:1, :, t0:t0 + TT].partition_broadcast(128), q=b.qg,
                       r=["d:csp"], w=["cbc"])

                for w0 in range(0, NDT, 6):
                    wave = list(range(w0, min(w0 + 6, NDT)))
                    for d in wave:
                        def dtmm(t, d=d, pacc=PB[d % 6]):
                            return t.matmul(pacc[:, 0:TT],
                                            dtw[:, d * 128:(d + 1) * 128],
                                            xdt[:, 0:TT], start=True, stop=True)
                        b.te(dtmm, r=["xdt"], w=[PBN[d % 6]])
                    for j, d in enumerate(wave):
                        rdep = [PBN[dd % 6] for dd in wave] if j == 0 else [PBN[d % 6]]

                        def sp1(s, d=d, pacc=PB[d % 6]):
                            return s.activation(out=del32[:, d * TT:(d + 1) * TT],
                                                in_=pacc[:, 0:TT], func=AF.Exp,
                                                scale=onecol[:],
                                                bias=dtb[:, d:d + 1])
                        b.sc(sp1, r=rdep, w=[DL[d]])
                    if w0 + 6 >= NDT // 2 and w0 < NDT // 2:
                        hh = NDT // 2 * TT
                        b.sc(lambda s: s.activation(
                            out=del32[:, 0:hh], in_=del32[:, 0:hh],
                            func=AF.Ln, scale=onecol[:], bias=onecol[:]),
                             r=DL[:NDT // 2], w=DL[:NDT // 2])
                hh = NDT // 2 * TT
                b.sc(lambda s: s.activation(
                    out=del32[:, hh:], in_=del32[:, hh:],
                    func=AF.Ln, scale=onecol[:], bias=onecol[:]),
                     r=DL[NDT // 2:], w=DL[NDT // 2:])

            def emit_dloop(it, pump):
                mt = mtile[:, (it % 2) * TT:(it % 2) * TT + TT]
                mtn = "mt%d" % (it % 2)
                pidx = [0]

                def drain(limit):
                    while pidx[0] < len(pump) and pump[pidx[0]][0] <= limit:
                        pump[pidx[0]][1]()
                        pidx[0] += 1

                def emit_tail(d):
                    def ts3(v):
                        return v.tensor_tensor(
                            out=hcb[:, 0:2 * TT].rearrange("p (s t) -> p s t", t=TT),
                            in0=hcb[:, 0:4 * TT]
                            .rearrange("p (s t) -> p s t", t=2 * TT)[:, :, 0:TT],
                            in1=hcb[:, 0:4 * TT]
                            .rearrange("p (s t) -> p s t", t=2 * TT)[:, :, TT:2 * TT],
                            op=ALU.add)
                    b.ve(ts3, r=["hcb"], w=["hcb"])
                    b.ve(lambda v: v.tensor_tensor(
                        out=ysum[:], in0=hcb[:, 0:TT], in1=hcb[:, TT:2 * TT],
                        op=ALU.add),
                         r=["hcb"], w=["ysum"])
                    b.ve(lambda v, d=d: v.scalar_tensor_tensor(
                        out=yt1[:], in0=xm16[:, d * TT:(d + 1) * TT],
                        scalar=dcol[:, d:d + 1], in1=ysum[:],
                        op0=ALU.mult, op1=ALU.add),
                         r=["xm16", "ysum"], w=["yt1"])
                    b.ve(lambda v, d=d: v.tensor_tensor(
                        out=yfin[:, d * TT:(d + 1) * TT], in0=yt1[:],
                        in1=silu_r[:, d * TT:(d + 1) * TT], op=ALU.mult),
                         r=["yt1", SL[d]], w=["yf%d" % d])
                    stage = ostage if d % 2 == 0 else ostage2
                    osn = "os0" if d % 2 == 0 else "os1"
                    nxt = ostage2 if d % 2 == 0 else ostage
                    nxo = "os1" if d % 2 == 0 else "os0"
                    if d + 1 < NDT:
                        b.load(nxt[:], out_wTi[(d + 1) * 128:(d + 2) * 128, :],
                               q=b.qg, w=[nxo])

                    def omm(t, k=d, stage=stage):
                        last = None
                        for tc in range(TCH):
                            for ng in range(2):
                                pD = (pO1, pO2, pO3, pO4)[tc * 2 + ng]
                                last = t.matmul(
                                    pD[:, 0:512],
                                    yfin[:, k * TT + tc * 128:k * TT + (tc + 1) * 128],
                                    stage[:, ng * 512:(ng + 1) * 512],
                                    start=(k == 0), stop=False)
                        return last

                    def omm_stage(omm=omm, d=d, osn=osn):
                        b.te(omm, r=[osn, "yf%d" % d] +
                             (["pb2", "pb3", "pb4", "pb5"] if d else []),
                             w=["pb2", "pb3", "pb4", "pb5"])
                    if OMM_TAIL:
                        omm_stage()
                    else:
                        deferred_omm.append(omm_stage)

                def emit_dact(d):
                    dabX = dab if d % 2 == 0 else dab2
                    dnm = "dab0" if d % 2 == 0 else "dab1"

                    def dact(s, d=d, dabX=dabX):
                        last = None
                        for s_ in range(DS):
                            last = s.activation(
                                out=dabX[:, s_ * BL + 2:s_ * BL + 2 + TT],
                                in_=del32[:, d * TT:(d + 1) * TT],
                                func=AF.Exp,
                                scale=asb[:, d * DS + s_:d * DS + s_ + 1],
                                bias=zerocol[:])
                        return last
                    b.sc(dact, r=[DL[d]], w=[dnm])

                prev = None
                deferred_omm = []
                b.load(ostage[:], out_wTi[0:128, :], q=b.qg, w=["os0"])
                emit_dact(0)
                emit_dact(1)
                for d in range(NDT):
                    dabX, dnm = (dab, "dab0") if d % 2 == 0 else (dab2, "dab1")
                    hX, hn = (hbb, "hbb0") if d % 2 == 0 else (hbb2, "hbb1")

                    b.ve(lambda v, d=d, mt=mt: v.tensor_tensor(
                        out=dtmp[:], in0=del32[:, d * TT:(d + 1) * TT],
                        in1=mt, op=ALU.mult),
                         r=[DL[d], mtn], w=["dtmp"])
                    b.ve(lambda v, d=d: v.tensor_tensor(
                        out=ys16[:], in0=dtmp[:],
                        in1=xm16[:, d * TT:(d + 1) * TT], op=ALU.mult),
                         r=["dtmp", "xm16"], w=["ys16"])

                    def dbu(v, d=d):
                        v.tensor_tensor(
                            out=dbb[:].rearrange("p (s c) -> p s c", c=BL)
                            [:, :, 2:2 + TT],
                            in0=ys16[:, None, :].broadcast_to([128, DS, TT]),
                            in1=bbc[:].rearrange("p (s t) -> p s t", t=TT),
                            op=ALU.mult)
                        return v.tensor_copy(
                            dbb[:].rearrange("p (s c) -> p s c", c=BL)[:, :, 0:1],
                            carry[:, d * DS:(d + 1) * DS]
                            .rearrange("p (s c) -> p s c", c=1))
                    b.ve(dbu, r=["ys16", "bbc", "carry"], w=["dbb"])

                    b.ve(lambda v, dabX=dabX, hX=hX: v.tensor_tensor_scan(
                        out=hX[:], data0=dabX[:], data1=dbb[:], initial=0.0,
                        op0=ALU.mult, op1=ALU.add),
                         r=[dnm, "dbb"], w=[hn])

                    def carr(v, d=d, hX=hX):
                        return v.tensor_copy(
                            carry[:, d * DS:(d + 1) * DS].rearrange("p (s c) -> p s c", c=1),
                            hX[:].rearrange("p (s c) -> p s c", c=BL)[:, :, BL - 1:BL])
                    b.ve(carr, r=[hn], w=["carry"])
                    if d + 2 < NDT:
                        emit_dact(d + 2)

                    if prev is not None:
                        emit_tail(prev)

                    def cmg(g, hX=hX):
                        return g.tensor_tensor(
                            out=hcb[:].rearrange("p (s t) -> p s t", t=TT),
                            in0=hX[:].rearrange("p (s c) -> p s c", c=BL)
                            [:, :, 2:2 + TT],
                            in1=cbc[:].rearrange("p (s t) -> p s t", t=TT),
                            op=ALU.mult)
                    b.ve(cmg, r=[hn, "cbc"], w=["hcb"])

                    def ts1(g):
                        return g.tensor_tensor(
                            out=hcb[:, 0:8 * TT].rearrange("p (s t) -> p s t", t=TT),
                            in0=hcb[:].rearrange("p (s t) -> p s t", t=2 * TT)
                            [:, :, 0:TT],
                            in1=hcb[:].rearrange("p (s t) -> p s t", t=2 * TT)
                            [:, :, TT:2 * TT],
                            op=ALU.add)
                    b.ve(ts1, r=["hcb"], w=["hcb"])

                    def ts2(g):
                        return g.tensor_tensor(
                            out=hcb[:, 0:4 * TT].rearrange("p (s t) -> p s t", t=TT),
                            in0=hcb[:, 0:8 * TT]
                            .rearrange("p (s t) -> p s t", t=2 * TT)[:, :, 0:TT],
                            in1=hcb[:, 0:8 * TT]
                            .rearrange("p (s t) -> p s t", t=2 * TT)[:, :, TT:2 * TT],
                            op=ALU.add)
                    b.ve(ts2, r=["hcb"], w=["hcb"])
                    if PUMP:
                        drain(d)
                    prev = d
                emit_tail(prev)
                for fn in deferred_omm:
                    fn()
                drain(NDT)

            def emit_merge(it):
                t0 = it * TT

                def obias(t):
                    last = None
                    for tc in range(TCH):
                        for ng in range(2):
                            pD = (pO1, pO2, pO3, pO4)[tc * 2 + ng]
                            last = t.matmul(pD[:, 0:512], ones3[0:1, :],
                                            outb2[0:1, ng * 512:(ng + 1) * 512],
                                            start=False, stop=True)
                    return last
                b.te(obias, r=["pb2", "pb3", "pb4", "pb5"],
                     w=["pb2", "pb3", "pb4", "pb5"])

                for tc in range(TCH):
                    pL = (pO1, pO3)[tc]
                    pR = (pO2, pO4)[tc]
                    if call == 1:
                        b.load(xres[:], src_t[t0 + tc * 128:t0 + (tc + 1) * 128, :],
                               q=b.qs, w=["xres"])

                        def merge(v, pL=pL, pR=pR):
                            v.tensor_tensor(out=x1tm[:, 0:512], in0=xres[:, 0:512],
                                            in1=pL[:, 0:512], op=ALU.add)
                            return v.tensor_tensor(out=x1tm[:, 512:1024],
                                                   in0=xres[:, 512:1024],
                                                   in1=pR[:, 0:512], op=ALU.add)
                        b.ve(merge, r=["pb2", "pb3", "pb4", "pb5", "xres"],
                             w=["x1tm"])
                        b.store(x1sp[t0 + tc * 128:t0 + (tc + 1) * 128, :], x1tm[:],
                                r=["x1tm"], w=["d:x1sp"])
                    else:
                        def merge2(sc, pL=pL, pR=pR):
                            sc.activation(out=x1tm[:, 0:512], in_=pL[:, 0:512],
                                          func=AF.Copy)
                            return sc.activation(out=x1tm[:, 512:1024],
                                                 in_=pR[:, 0:512], func=AF.Copy)
                        b.sc(merge2, r=["pb2", "pb3", "pb4", "pb5"], w=["x1tm"])
                        b.store(o2tm[t0 + tc * 128:t0 + (tc + 1) * 128, :], x1tm[:],
                                r=["x1tm"], w=["d:o2tm"])
                        cix = (t0 + tc * 128) // 128
                        b.sc(lambda sc, cix=cix: sc.activation(
                            out=o2bfb[:], in_=x1tm[:], func=AF.Copy,
                            scale=mcol[:, cix:cix + 1]),
                             r=["x1tm"], w=["o2bfb"])

                        def tr2(t):
                            last = None
                            for k in range(NMT):
                                last = t.transpose(pT[:, k * 128:(k + 1) * 128],
                                                   o2bfb[:, k * 128:(k + 1) * 128],
                                                   ident[:])
                            return last
                        b.te(tr2, r=["o2bfb"], w=["pT"])

                        b.sc(lambda sc: sc.activation(out=o2fmb[:], in_=pT[:, 0:DM],
                                                      func=AF.Copy),
                             r=["pT"], w=["o2bfb"])
                        b.store(o2fm[:, 1 + t0 + tc * 128:1 + t0 + (tc + 1) * 128]
                                .rearrange("(k p) m -> p k m", p=128),
                                o2fmb[:].rearrange("p (k m) -> p k m", m=128),
                                r=["o2bfb"], w=["d:o2fm"])

            for fn in produce_stages(0):
                fn[1]()
            for it in range(NT):
                emit_mid(it)
                nxt_stages = produce_stages(it + 1) if it + 1 < NT else []
                emit_dloop(it, nxt_stages)
                emit_merge(it)

        emit_call(1)
        emit_call(2)
        es2.close()

        wbig = E(nc.sbuf_tensor("wbig", [128, 5 * NMT * DM], BF16))
        w32stg = E(nc.sbuf_tensor("w32stg", [128, DM], F32))
        hi32stg = E(nc.sbuf_tensor("hi32stg", [128, DM], F32))
        lost1 = E(nc.sbuf_tensor("lost1", [128, DM], BF16))
        lost2 = E(nc.sbuf_tensor("lost2", [128, DM], BF16))
        xalo = E(nc.sbuf_tensor("xalo", [128, DM], BF16))
        h2lo = E(nc.sbuf_tensor("h2lo", [128, DM], BF16))

        WT32 = E(nc.sbuf_tensor("WT32", [128, NMT * DM], F32))
        WTbf = E(nc.sbuf_tensor("WTbf", [128, NMT * DM], BF16))
        rkb = E(nc.sbuf_tensor("rkb", [128, NMT * MAC], BF16))
        wkb = E(nc.sbuf_tensor("wkb", [128, NMT * MAC], BF16))
        wktm = E(nc.sbuf_tensor("wktm", [128, DM], BF16))
        vbf = E(nc.sbuf_tensor("vbf", [128, DM], BF16))
        vgw = E(nc.sbuf_tensor("vgw", [128, DM], BF16))
        stp = E(nc.sbuf_tensor("stp", [MAC, MAC], F32))
        smt = E(nc.sbuf_tensor("smt", [MAC, MAC], BF16))
        rkg = E(nc.sbuf_tensor("rkg", [128, NMT * MAC], BF16))
        rkg_lo = xalo
        rdsb = E(nc.sbuf_tensor("rdsb", [128, NMT * MAC], BF16))
        attn = E(nc.sbuf_tensor("attn", [128, DM], F32))
        xa32 = E(nc.sbuf_tensor("xa32", [128, DM], F32))


        b.ve(lambda v: v.memset(WT32[:], 0.0))
        b.ve(lambda v: v.memset(WTbf[:], 0.0))
        wchunks = [(w, k, ap) for w, ap in enumerate([wW, wR, fW1, fW2, fW3])
                   for k in range(NMT)]
        wstg2 = xa32
        b.load(w32stg[:], wchunks[0][2][0:128, :], w=["w32stg"])
        for i, (w, k, ap) in enumerate(wchunks):
            stg, stn = (w32stg, "w32stg") if i % 2 == 0 else (wstg2, "xa32")
            nxt, nxn = (wstg2, "xa32") if i % 2 == 0 else (w32stg, "w32stg")
            if i + 1 < len(wchunks):
                w2, k2, ap2 = wchunks[i + 1]
                b.load(nxt[:], ap2[k2 * 128:(k2 + 1) * 128, :],
                       q=(b.ql if i % 2 == 0 else b.qg), w=[nxn])
            b.ve(lambda v, w=w, k=k, stg=stg: v.tensor_copy(W5(w, k), stg[:]),
                 r=[stn], w=["wbig"])
            b.ve(lambda v, w=w, k=k: v.tensor_copy(hi32stg[:], W5(w, k)),
                 r=["wbig"], w=["hi32"])
            b.ve(lambda v, stg=stg: v.tensor_tensor(out=lost1[:], in0=stg[:],
                                                    in1=hi32stg[:],
                                                    op=ALU.subtract),
                 r=[stn, "hi32"], w=["lost1"])
            b.store(wlo[w, k * 128:(k + 1) * 128, :], lost1[:],
                    r=["lost1"], w=["d:wlo"])
        reads32 = attn
        x1ch = hi32stg
        o2ch = w32stg
        xch = jsq
        for c in range(NCH):
            ct0 = c * MAC
            b.load(rkb[:].rearrange("p (k m) -> p k m", m=MAC),
                   o2fm[:, 1 + ct0:1 + ct0 + MAC]
                   .rearrange("(k p) m -> p k m", p=128),
                   r=["d:o2fm"], w=["rkb"])
            b.load(wkb[:].rearrange("p (k m) -> p k m", m=MAC),
                   o2fm[:, ct0:ct0 + MAC].rearrange("(k p) m -> p k m", p=128),
                   r=["d:o2fm"], w=["wkb"])

            def vmm(t):
                last = None
                for ng in range(2):
                    pD = (pO1, pO2)[ng]
                    for k in range(NMT):
                        last = t.matmul(pD[:, 0:512], rkb[:, k * MAC:(k + 1) * MAC],
                                        W5(0, k)[:, ng * 512:(ng + 1) * 512],
                                        start=(k == 0), stop=False)
                return last
            b.te(vmm, r=["rkb", "wbig"], w=["pO12"])
            b.load(lost1[:], wlo[0, 0:128, :], r=["d:wlo"], w=["lost1"])
            for k in range(NMT):
                stg = lost1 if k % 2 == 0 else lost2
                stn = "lost1" if k % 2 == 0 else "lost2"
                nxtstg = lost2 if k % 2 == 0 else lost1
                nxtn = "lost2" if k % 2 == 0 else "lost1"
                if k + 1 < NMT:
                    b.load(nxtstg[:], wlo[0, (k + 1) * 128:(k + 2) * 128, :],
                           r=["d:wlo"], w=[nxtn])

                def vlo(t, k=k, stg=stg):
                    last = None
                    for ng in range(2):
                        pD = (pO1, pO2)[ng]
                        last = t.matmul(pD[:, 0:512], rkb[:, k * MAC:(k + 1) * MAC],
                                        stg[:, ng * 512:(ng + 1) * 512],
                                        start=False, stop=(k == NMT - 1))
                    return last
                b.te(vlo, r=["rkb", stn, "pO12"], w=["pO12"])
            b.ve(lambda v: v.tensor_tensor(out=gwm[:], in0=gw_col[:],
                                           in1=mcol[:, c:c + 1], op=ALU.mult),
                 w=["gwm"], r=[])

            def vev(v):
                v.tensor_copy(vbf[:, 0:512], pO1[:, 0:512])
                v.tensor_copy(vbf[:, 512:1024], pO2[:, 0:512])
                v.tensor_scalar(out=vgw[:, 0:512], in0=pO1[:, 0:512],
                                scalar1=gwm[:], scalar2=None, op0=ALU.mult)
                return v.tensor_scalar(out=vgw[:, 512:1024], in0=pO2[:, 0:512],
                                       scalar1=gwm[:], scalar2=None, op0=ALU.mult)
            b.ve(vev, r=["pO12", "gwm"], w=["vbf", "vgw"])

            def wtr(t):
                last = None
                for mt in range(NMT):
                    last = t.transpose(pT[:, mt * 128:(mt + 1) * 128],
                                       wkb[:, mt * MAC:(mt + 1) * MAC], ident[:])
                return last
            b.te(wtr, r=["wkb"], w=["pT"])
            b.ve(lambda v: v.tensor_copy(wktm[:], pT[:, 0:DM]),
                 r=["pT"], w=["wktm"])

            if c >= WCH0:
                def stmm(t):
                    last = None
                    for k in range(NMT):
                        last = t.matmul(pA[:, 0:MAC], wkb[:, k * MAC:(k + 1) * MAC],
                                        rkb[:, k * MAC:(k + 1) * MAC],
                                        start=(k == 0), stop=(k == NMT - 1))
                    return last
                b.te(stmm, r=["wkb", "rkb"], w=["pA"])
                b.ve(lambda v: v.tensor_copy(stp[:], pA[:, 0:MAC]),
                     r=["pA"], w=["stp"])
                b.ve(lambda v: v.tensor_tensor(out=smt[:], in0=stp[:], in1=mTt[:],
                                               op=ALU.mult),
                     r=["stp"], w=["smt"])
                def rg(v):
                    last = None
                    for k in range(NMT):
                        last = v.tensor_tensor(out=rkg[:, k * MAC:(k + 1) * MAC],
                                               in0=rkb[:, k * MAC:(k + 1) * MAC],
                                               in1=gin_bc[:], op=ALU.mult)
                    return last
                b.ve(rg, r=["rkb"], w=["rkg"])

                def rg2(v):
                    last = None
                    for k in range(NMT):
                        last = v.tensor_tensor(out=rkg_lo[:, k * MAC:(k + 1) * MAC],
                                               in0=rkb[:, k * MAC:(k + 1) * MAC],
                                               in1=gin_lo[:], op=ALU.mult)
                    return last
                b.ve(rg2, r=["rkb"], w=["xalo"])

                def ii(t):
                    last = None
                    for dt_ in range(NMT):
                        pD = (pO3, pO4)[dt_ // 4]
                        sl = slice((dt_ % 4) * 128, (dt_ % 4) * 128 + 128)
                        for et in range(NMT):
                            t.matmul(pD[:, sl],
                                     WTbf[:, (et * NMT + dt_) * 128:
                                          (et * NMT + dt_) * 128 + 128],
                                     rkg[:, et * MAC:(et + 1) * MAC],
                                     start=(et == 0), stop=False)
                            t.matmul(pD[:, sl],
                                     WTbf[:, (et * NMT + dt_) * 128:
                                          (et * NMT + dt_) * 128 + 128],
                                     rkg_lo[:, et * MAC:(et + 1) * MAC],
                                     start=False, stop=False)
                        last = t.matmul(pD[:, sl], vbf[:, dt_ * 128:(dt_ + 1) * 128],
                                        smt[:], start=False, stop=True)
                    return last
                b.te(ii, r=["WTbf", "rkg", "xalo", "vbf", "smt"], w=["pO34"])

                def rev(v):
                    v.tensor_copy(reads32[:, 0:512], pO3[:, 0:512])
                    return v.tensor_copy(reads32[:, 512:1024], pO4[:, 0:512])
                b.ve(rev, r=["pO34"], w=["attn"])
                b.ve(lambda v: v.tensor_copy(rdsb[:], reads32[:]),
                     r=["attn"], w=["rdsb"])

            for et in range(NMT):
                pU1 = pO1 if et % 2 == 0 else pO3
                pU2 = pO2 if et % 2 == 0 else pO4
                pun = "pO12" if et % 2 == 0 else "pO34"

                def wmm(t, et=et, pU1=pU1, pU2=pU2):
                    t.matmul(pU1[:, 0:512], wktm[:, et * 128:(et + 1) * 128],
                             vgw[:, 0:512], start=True, stop=True)
                    return t.matmul(pU2[:, 0:512], wktm[:, et * 128:(et + 1) * 128],
                                    vgw[:, 512:1024], start=True, stop=True)
                b.te(wmm, r=["wktm", "vgw"], w=[pun])

                def wupd(v, et=et, pU1=pU1, pU2=pU2):
                    base = et * DM
                    v.scalar_tensor_tensor(out=WT32[:, base:base + 512],
                                           in0=WT32[:, base:base + 512],
                                           scalar=gc_col[:], in1=pU1[:, 0:512],
                                           op0=ALU.mult, op1=ALU.add)
                    return v.scalar_tensor_tensor(out=WT32[:, base + 512:base + 1024],
                                                  in0=WT32[:, base + 512:base + 1024],
                                                  scalar=gc_col[:], in1=pU2[:, 0:512],
                                                  op0=ALU.mult, op1=ALU.add)
                b.ve(wupd, r=[pun, "WT32"], w=["WT32"])
            b.sc(lambda sc: sc.activation(out=WTbf[:], in_=WT32[:], func=AF.Copy),
                 r=["WT32"], w=["WTbf"])

            if c < WCH0:
                continue

            if c == WCH0 and dbg:
                DUMP("reads32", reads32[:])
                DUMP("rkb", rkb[:])
                DUMP("vbf", vbf[:])
                DUMP("smt", smt[:])
                DUMP("rdsb", rdsb[:])
                DUMP("WTbf", WTbf[:, 0:4096])
            def rpmm(t):
                last = None
                for ng in range(2):
                    pD = (pO1, pO2)[ng]
                    for k in range(NMT):
                        last = t.matmul(pD[:, 0:512], rdsb[:, k * MAC:(k + 1) * MAC],
                                        W5(1, k)[:, ng * 512:(ng + 1) * 512],
                                        start=(k == 0), stop=(k == NMT - 1))
                return last
            b.te(rpmm, r=["rdsb", "wbig"], w=["pO12"])
            b.load(xch[:], xin[ct0:ct0 + MAC, :], r=[], w=["jsq"])
            b.load(x1ch[:], x1sp[ct0:ct0 + MAC, :], r=["d:x1sp"], w=["hi32"])
            b.load(o2ch[:], o2tm[ct0:ct0 + MAC, :], r=["d:o2tm"], w=["w32stg"])

            def mg(v):
                v.scalar_tensor_tensor(out=attn[:, 0:512], in0=pO1[:, 0:512],
                                       scalar=ALPHA, in1=x1ch[:, 0:512],
                                       op0=ALU.mult, op1=ALU.add)
                return v.scalar_tensor_tensor(out=attn[:, 512:1024], in0=pO2[:, 0:512],
                                              scalar=ALPHA, in1=x1ch[:, 512:1024],
                                              op0=ALU.mult, op1=ALU.add)
            b.ve(mg, r=["pO12", "hi32"], w=["attn"])
            b.ve(lambda v: v.tensor_tensor(out=xa32[:], in0=attn[:], in1=xch[:],
                                           op=ALU.add),
                 r=["attn", "jsq"], w=["xa32"])
            b.ve(lambda v: v.tensor_tensor(out=attn[:], in0=xa32[:], in1=o2ch[:],
                                           op=ALU.add),
                 r=["xa32", "w32stg"], w=["attn"])

            if c == WCH0 and dbg:
                DUMP("attn", attn[:])

            def ln(inb_, outf, inn, outn, outb_=None, obn=None):
                b.sc(lambda s: s.activation(out=jsq[:], in_=inb_[:], func=AF.Square),
                     r=[inn], w=["jsq"])
                b.ve(lambda v: v.reduce_sum(col1[:], inb_[:], axis=mybir.AxisListType.X),
                     r=[inn], w=["c1"])
                b.ve(lambda v: v.reduce_sum(col2[:], jsq[:], axis=mybir.AxisListType.X),
                     r=["jsq"], w=["c2"])
                b.ve(lambda v: v.tensor_scalar(out=col3[:], in0=col1[:],
                                               scalar1=1.0 / DM, scalar2=None,
                                               op0=ALU.mult),
                     r=["c1"], w=["c3"])
                b.ve(lambda v: v.tensor_scalar(out=col4[:], in0=col2[:],
                                               scalar1=1.0 / DM, scalar2=None,
                                               op0=ALU.mult),
                     r=["c2"], w=["c4"])
                b.ve(lambda v: v.tensor_tensor(out=col5[:], in0=col3[:], in1=col3[:],
                                               op=ALU.mult),
                     r=["c3"], w=["c5"])
                b.ve(lambda v: v.tensor_tensor(out=col2[:], in0=col4[:], in1=col5[:],
                                               op=ALU.subtract),
                     r=["c4", "c5"], w=["c2"])
                b.sc(lambda s: s.activation(out=col4[:], in_=col2[:], func=AF.Sqrt,
                                            bias=epscol[:]),
                     r=["c2"], w=["c4"])
                b.ve(lambda v: v.reciprocal(out=col5[:], in_=col4[:]),
                     r=["c4"], w=["c5"])
                b.ve(lambda v: v.tensor_scalar(out=outf[:], in0=inb_[:],
                                               scalar1=col3[:], scalar2=col5[:],
                                               op0=ALU.subtract, op1=ALU.mult),
                     r=[inn, "c3", "c5"], w=[outn])
                if outb_ is not None:
                    b.ve(lambda v: v.tensor_copy(outb_[:], outf[:]),
                         r=[outn], w=[obn])

            xabf = rkg
            xafm = wktm
            xafm_lo = vgw
            h1fm = rkb
            h1lo = vbf
            h2fm = wkb
            ln(attn, xa32, "attn", "xa32", xabf, "rkg")
            b.ve(lambda v: v.tensor_copy(hi32stg[:], xabf[:]),
                 r=["rkg"], w=["hi32"])
            b.ve(lambda v: v.tensor_tensor(out=xalo[:], in0=xa32[:], in1=hi32stg[:],
                                           op=ALU.subtract),
                 r=["xa32", "hi32"], w=["xalo"])

            def ftr(t):
                last = None
                for mt in range(NMT):
                    last = t.transpose(pT[:, mt * 128:(mt + 1) * 128],
                                       xabf[:, mt * 128:(mt + 1) * 128], ident[:])
                return last
            b.te(ftr, r=["rkg"], w=["pT"])
            b.ve(lambda v: v.tensor_copy(xafm[:], pT[:, 0:DM]),
                 r=["pT"], w=["wktm"])

            def ftr2(t):
                last = None
                for mt in range(NMT):
                    last = t.transpose(pT[:, mt * 128:(mt + 1) * 128],
                                       xalo[:, mt * 128:(mt + 1) * 128], ident[:])
                return last
            b.te(ftr2, r=["xalo"], w=["pT"])
            b.ve(lambda v: v.tensor_copy(xafm_lo[:], pT[:, 0:DM]),
                 r=["pT"], w=["vgw"])

            def fmm1(t):
                last = None
                for m in range(NMT):
                    pD = (pO3, pO4)[m // 4]
                    sl = slice((m % 4) * 128, (m % 4) * 128 + 128)
                    for k in range(NMT):
                        t.matmul(pD[:, sl], W5(2, k)[:, m * 128:(m + 1) * 128],
                                 xafm[:, k * 128:(k + 1) * 128],
                                 start=(k == 0), stop=False)
                        last = t.matmul(pD[:, sl], W5(2, k)[:, m * 128:(m + 1) * 128],
                                        xafm_lo[:, k * 128:(k + 1) * 128],
                                        start=False, stop=(k == NMT - 1))
                return last
            b.te(fmm1, r=["wktm", "vgw", "wbig"], w=["pO34"])

            def fe1(v):
                v.tensor_copy(h1fm[:, 0:512], pO3[:, 0:512])
                return v.tensor_copy(h1fm[:, 512:1024], pO4[:, 0:512])
            b.ve(fe1, r=["pO34"], w=["rkb"])
            b.ve(lambda v: v.tensor_copy(hi32stg[:, 0:512], pO3[:, 0:512]),
                 r=["pO34"], w=["hi32"])
            b.ve(lambda v: v.tensor_copy(hi32stg[:, 512:1024], pO4[:, 0:512]),
                 r=["pO34", "hi32"], w=["hi32"])
            b.ve(lambda v: v.tensor_copy(w32stg[:], h1fm[:]),
                 r=["rkb"], w=["w32stg"])
            b.ve(lambda v: v.tensor_tensor(out=h1lo[:], in0=hi32stg[:], in1=w32stg[:],
                                           op=ALU.subtract),
                 r=["hi32", "w32stg"], w=["vbf"])

            def fmm2(t):
                last = None
                for m in range(NMT):
                    pD = (pO3, pO4)[m // 4]
                    sl = slice((m % 4) * 128, (m % 4) * 128 + 128)
                    for k in range(NMT):
                        t.matmul(pD[:, sl], W5(3, k)[:, m * 128:(m + 1) * 128],
                                 h1fm[:, k * 128:(k + 1) * 128],
                                 start=(k == 0), stop=False)
                        last = t.matmul(pD[:, sl], W5(3, k)[:, m * 128:(m + 1) * 128],
                                        h1lo[:, k * 128:(k + 1) * 128],
                                        start=False, stop=(k == NMT - 1))
                return last
            b.te(fmm2, r=["rkb", "vbf", "wbig"], w=["pO34"])

            def fe2(v):
                v.tensor_copy(h2fm[:, 0:512], pO3[:, 0:512])
                return v.tensor_copy(h2fm[:, 512:1024], pO4[:, 0:512])
            b.ve(fe2, r=["pO34"], w=["wkb"])
            b.ve(lambda v: v.tensor_copy(hi32stg[:, 0:512], pO3[:, 0:512]),
                 r=["pO34"], w=["hi32"])
            b.ve(lambda v: v.tensor_copy(hi32stg[:, 512:1024], pO4[:, 0:512]),
                 r=["pO34", "hi32"], w=["hi32"])
            b.ve(lambda v: v.tensor_copy(w32stg[:], h2fm[:]),
                 r=["wkb"], w=["w32stg"])
            b.ve(lambda v: v.tensor_tensor(out=h2lo[:], in0=hi32stg[:], in1=w32stg[:],
                                           op=ALU.subtract),
                 r=["hi32", "w32stg"], w=["h2lo"])

            def fmm3(t):
                last = None
                for ng in range(2):
                    pD = (pO1, pO2)[ng]
                    for k in range(NMT):
                        t.matmul(pD[:, 0:512], h2fm[:, k * 128:(k + 1) * 128],
                                 W5(4, k)[:, ng * 512:(ng + 1) * 512],
                                 start=(k == 0), stop=False)
                        t.matmul(pD[:, 0:512], h2lo[:, k * 128:(k + 1) * 128],
                                 W5(4, k)[:, ng * 512:(ng + 1) * 512],
                                 start=False, stop=False)
                    last = t.matmul(pD[:, 0:512], ones3[0:1, :],
                                    fbr2[0:1, ng * 512:(ng + 1) * 512],
                                    start=False, stop=True)
                return last
            b.te(fmm3, r=["wkb", "h2lo", "wbig"], w=["pO12"])

            def tnh(s):
                s.activation(out=attn[:, 0:512], in_=pO1[:, 0:512], func=AF.Tanh)
                return s.activation(out=attn[:, 512:1024], in_=pO2[:, 0:512],
                                    func=AF.Tanh)
            b.sc(tnh, r=["pO12"], w=["attn"])
            b.ve(lambda v: v.tensor_tensor(out=o2ch[:], in0=attn[:], in1=xa32[:],
                                           op=ALU.add),
                 r=["attn", "xa32"], w=["w32stg"])
            ln(o2ch, xch, "w32stg", "jsq")
            if c == WCH0 and dbg:
                DUMP("xa32", xa32[:])
                DUMP("ln2in", o2ch[:])
                DUMP("fin", xch[:])
            b.store(o[(c - WCH0) * MAC:(c - WCH0 + 1) * MAC, :], xch[:],
                    r=["jsq"], w=["d:o"])

        if dbg:
            b.store(dbg_aps["d_x1"][:, :], x1sp[:, :])
            b.store(dbg_aps["d_o2tm"][:, :], o2tm[:, :])
            b.store(dbg_aps["d_o2fm"][:, :], o2fm[:, :])

        b.finish()
        dump_log = dump_state["log"]
    return nc, {"dbg": dbg_names, "log": dump_log}


def prep_shared(ins):
    bf = ml_dtypes.bfloat16
    g = lambda k: np.ascontiguousarray(np.asarray(ins["L0_attn_" + k], np.float32))
    f = lambda k: np.ascontiguousarray(np.asarray(ins["L0_ffn_" + k], np.float32))
    d = {}
    inwt = np.ascontiguousarray(g("in_proj_w").T).astype(bf)
    d["in_wTr"] = np.ascontiguousarray(
        inwt.reshape(NMT, 128, NOC, 128).transpose(2, 1, 0, 3)
        .reshape(NOC * 128, NMT * 128))
    d["in_b_row"] = g("in_proj_b")[None, :].astype(bf)
    d["inb_c"] = np.ascontiguousarray(g("in_proj_b").reshape(NOC, 128).T)
    d["inb_r32"] = g("in_proj_b")[None, :].copy()
    d["outb32"] = g("out_proj_b")[None, :].copy()
    d["fb32"] = np.ascontiguousarray(np.asarray(ins["L0_ffn_bias"], np.float32))[None, :]
    d["ones32"] = np.ones((1, 128), np.float32)
    cw = g("conv_w")[:, 0, :]
    d["convw_c"] = np.ascontiguousarray(cw.reshape(NDT, 128, 4).transpose(1, 0, 2)
                                        .reshape(128, NDT * 4))
    d["convb_c"] = np.ascontiguousarray(g("conv_b").reshape(NDT, 128).T)
    xp = g("x_proj_w")
    pad = np.zeros((DI, 128), np.float32)
    pad[:, 0:DS] = xp[DR:DR + DS].T
    pad[:, 32:32 + DS] = xp[DR + DS:DR + 2 * DS].T
    pad[:, 64:128] = xp[0:DR].T
    d["xp_pad"] = pad
    d["dt_wTi"] = np.ascontiguousarray(g("dt_proj_w").T).astype(bf)
    d["dtb_c"] = np.ascontiguousarray(g("dt_proj_b").reshape(NDT, 128).T)
    d["alog_r"] = np.ascontiguousarray(g("A_log").reshape(NDT, 128, DS)
                                       .transpose(1, 0, 2).reshape(128, NDT * DS))
    d["d_c"] = np.ascontiguousarray(g("D_ssm").reshape(NDT, 128).T)
    d["nw1_c"] = np.ascontiguousarray(g("norm1_w").reshape(NMT, 128).T)
    d["nw2_c"] = np.ascontiguousarray(g("norm2_w").reshape(NMT, 128).T)
    d["out_wTi"] = np.ascontiguousarray(g("out_proj_w").T).astype(bf)
    d["out_b_row"] = g("out_proj_b")[None, :].astype(bf)
    d["wW"] = np.ascontiguousarray(g("proj_write_w").T)
    d["wR"] = np.ascontiguousarray(g("proj_read_w").T)
    d["fW1"] = f("weight")
    d["fW2"] = f("aux_weight1")
    d["fW3"] = f("aux_weight2")
    d["fb_row"] = f("bias")[None, :].astype(bf)
    d["decay"] = np.asarray(ins["L0_attn_decay"], np.float32).reshape(1, 1)
    p = np.arange(MAC)
    d["diffsT"] = np.ascontiguousarray(
        np.clip(p[None, :] - 1 - p[:, None], 0, None).astype(np.float32))
    d["causT"] = np.ascontiguousarray((p[None, :] > p[:, None]).astype(np.float32))
    d["p_row"] = p.astype(np.float32)[None, :]
    d["cp_col"] = (MAC - 1 - p).astype(np.float32)[:, None]
    d["ident_in"] = np.eye(128, dtype=np.float32).astype(bf)
    d["ones_row"] = np.ones((1, 128), np.float32).astype(bf)
    return d


def prep_core(ins, T, bidx, role):
    bf = ml_dtypes.bfloat16
    x = np.asarray(ins["L0_attn_x"], np.float32)[bidx]
    H = T // 2
    if role == 0:
        xbuf = np.concatenate([np.zeros((H, DM), np.float32), x[:H]], 0)
        m = np.concatenate([np.zeros(H, np.float32), np.ones(H, np.float32)])
    else:
        xbuf = x[:T]
        m = np.ones(T, np.float32)
    d = {"xin": np.ascontiguousarray(xbuf),
         "mrow_16": m[None, :].astype(np.float16),
         "mcols": np.ascontiguousarray(m.reshape(T // 128, 128).T)}
    return d


_CACHE = {}


def kernel(**inputs):
    T = 4096
    if "nc" not in _CACHE:
        nc, _ = build(T)
        _CACHE["nc"] = nc
    nc = _CACHE["nc"]
    shared = prep_shared(inputs)
    in_maps = []
    for c in range(8):
        m = dict(shared)
        m.update(prep_core(inputs, T, c // 2, c % 2))
        in_maps.append(m)
    res = run_bass_kernel_spmd(nc, in_maps, core_ids=list(range(8)))
    out = np.empty((4, T, DM), np.float32)
    for bidx in range(4):
        out[bidx, :T // 2] = np.asarray(res.results[2 * bidx]["o"])
        out[bidx, T // 2:] = np.asarray(res.results[2 * bidx + 1]["o"])
    return out
